# version 04 (final)
# speedup vs baseline: 1.0613x; 1.0068x over previous
import math
import numpy as np
import ml_dtypes
import concourse.bass as bass
import concourse.mybir as mybir
from concourse.bass_utils import run_bass_kernel_spmd

F32 = mybir.dt.float32
BF16 = mybir.dt.bfloat16
AF = mybir.ActivationFunctionType
ALU = mybir.AluOpType
AX = mybir.AxisListType

D = 1024
NCH = 8
T = 2048
S = 4096
TT = 512
NTT = T // TT
NKC = S // 128
BIG = 1.0e9
LN_EPS = 1e-5

ENGS = ("tensor", "vector", "scalar", "gpsimd", "sync")

bfnp = lambda a: np.ascontiguousarray(np.asarray(a), dtype=ml_dtypes.bfloat16)
f32np = lambda a: np.ascontiguousarray(np.asarray(a), dtype=np.float32)


class Sem:
    __slots__ = ("h", "n")
    def __init__(self, nc, name):
        self.h = nc.alloc_semaphore(name)
        self.n = 0


class EngProxy:
    def __init__(self, ctx, name):
        self._ctx = ctx
        self._name = name

    def wait(self, sem, val=None):
        v = sem.n if val is None else val
        h = sem.h
        if v > 0:
            self._ctx.streams[self._name].append(lambda e: e.wait_ge(h, v))

    def raw(self, fn):
        self._ctx.streams[self._name].append(fn)

    def bump(self, sem, by=1):
        sem.n += by
        h, b = sem.h, by
        self._ctx.streams[self._name].append(lambda e: e.sem_inc(h, b))

    def __getattr__(self, meth):
        ctx, name = self._ctx, self._name
        def call(*a, inc=None, inc_by=1, **k):
            if inc is not None:
                inc.n += inc_by
                ih, iv = inc.h, inc_by
                ctx.last_inc[name] = (inc, inc.n)
            def f(e, a=a, k=k):
                r = getattr(e, meth)(*a, **k)
                if inc is not None:
                    r.then_inc(ih, iv)
            ctx.streams[name].append(f)
        return call


class Ctx:
    def __init__(self, nc):
        self.nc = nc
        self.streams = {e: [] for e in ENGS}
        self.t = EngProxy(self, "tensor")
        self.v = EngProxy(self, "vector")
        self.s = EngProxy(self, "scalar")
        self.g = EngProxy(self, "gpsimd")
        self.y = EngProxy(self, "sync")
        self.eng = {"tensor": self.t, "vector": self.v, "scalar": self.s,
                    "gpsimd": self.g, "sync": self.y}
        self.wq = self.y
        self._sems = {}
        self.last_inc = {e: None for e in ENGS}
        self.gdma = Sem(nc, "gdma")
        self.sdma = Sem(nc, "sdma")
        self._barsem = Sem(nc, "barrier")
        self._uid = 0
        self.base = nc.sbuf_base
        self.limit = nc.sbuf_top
        self._fixed_off = self.base
        self._scr0 = None
        self._scr = None
        self.consts = {}
        self._const_cols = 0
        self.CONSTS = self.fixed("CONSTS", [128, 64], F32)
        self.WSLOT = [self.fixed(f"WSLOT{i}", [128, 8192], mybir.dt.uint8)
                      for i in range(2)]
        self._wload = [0, 0]
        self.PG = [nc.alloc_psum_tensor(self.uname("pg"), [128, 2048], F32)
                   for _ in range(2)]
        self.banks = [self.PG[i][:, j * 512:(j + 1) * 512]
                      for i in range(2) for j in range(4)]
        self.bank_guard = {}

    def bank_acquire(self, bis, eng="tensor"):
        for bi in bis:
            g = self.bank_guard.get(bi)
            if g is not None:
                sem, cnt = g
                self.eng[eng].wait(sem, cnt)

    def bank_release(self, bis, sem):
        for bi in bis:
            self.bank_guard[bi] = (sem, sem.n)

    def uname(self, n):
        self._uid += 1
        return f"{n}_{self._uid}"

    def _alloc_at(self, name, shape, dtype, off):
        nbytes = int(np.prod(shape[1:])) * mybir.dt.size(dtype)
        assert off + nbytes <= self.limit, (
            f"SBUF overflow: {name} @ {off - self.base} + {nbytes} over "
            f"{self.limit - self.base}")
        h = self.nc.alloc_sbuf_tensor_at(self.uname(name), list(shape), dtype, offset=off)
        return h, off + nbytes

    def fixed(self, name, shape, dtype):
        off = (self._fixed_off + 511) // 512 * 512
        h, self._fixed_off = self._alloc_at(name, shape, dtype, off)
        return h

    def end_fixed(self):
        self._scr0 = (self._fixed_off + 511) // 512 * 512
        self._scr = self._scr0

    def scratch(self, name, shape, dtype):
        off = (self._scr + 127) // 128 * 128
        h, self._scr = self._alloc_at(name, shape, dtype, off)
        return h

    def scratch_mark(self):
        return self._scr

    def scratch_release(self, mark):
        self._scr = mark

    def sem(self, name):
        if name not in self._sems:
            self._sems[name] = Sem(self.nc, name)
        return self._sems[name]

    def constcol(self, value, dtype=F32):
        key = (str(dtype), float(value))
        if key in self.consts:
            return self.consts[key]
        assert self._const_cols < 64
        col = self._const_cols
        self._const_cols += 1
        ap = self.CONSTS[:, col:col + 1]
        if dtype != F32:
            ap = ap.bitcast(dtype)[:, 0:1]
        self.consts[key] = ap
        self.g.memset(ap, float(value))
        self.nc.const_aps.register(dtype, float(value), ap)
        return ap

    def statrow(self, width=512, dtype=F32, p0=True):
        h = self.scratch("stat", [1, width], dtype)
        return h[:]

    def stat_reset(self):
        pass

    def barrier(self, engines=ENGS):
        b = self._barsem
        base = b.n
        h = b.h
        for i, e in enumerate(engines):
            li = self.last_inc[e]
            if li is not None:
                self.eng[e].wait(li[0], li[1])
            if base + i > 0:
                self.streams[e].append(lambda E, h=h, v=base + i: E.wait_ge(h, v))
            self.streams[e].append(lambda E, h=h: E.nop().then_inc(h, 1))
        b.n = base + len(engines)
        v = b.n
        for e in engines:
            self.streams[e].append(lambda E, h=h, v=v: E.wait_ge(h, v))

    def finalize(self):
        nc = self.nc
        with nc.Block() as blk:
            for e in ENGS:
                fns = self.streams[e]
                getattr(blk, e)(lambda E, fns=fns: [f(E) for f in fns])
        return nc


def emit_ln(ctx, C, XRES, XN):
    t, v, s = ctx.t, ctx.v, ctx.s
    mS, vS, aS = ctx.sem("lnm"), ctx.sem("lnv"), ctx.sem("lna")
    mark = ctx.scratch_mark()
    XB = ctx.scratch("ln_xb", [128, 2 * TT], BF16)
    SQ = ctx.scratch("ln_sq", [128, 2 * TT], BF16)
    TMP = ctx.scratch("ln_tmp", [128, TT], F32)
    ctx.stat_reset()
    mu = ctx.statrow(); ex2 = ctx.statrow(); msq = ctx.statrow()
    std = ctx.statrow(); rstd = ctx.statrow(); nmu = ctx.statrow()
    bsum, bsq = ctx.banks[0][0:1, :], ctx.banks[1][0:1, :]
    brb, bmb = ctx.banks[2], ctx.banks[3]
    for tt in range(NTT):
        for c in range(NCH):
            sl = c % 2
            xa = XRES[:, c * T + tt * TT: c * T + (tt + 1) * TT]
            if c >= 2 or tt > 0:
                v.wait(mS)
                s.wait(mS)
            v.tensor_copy(XB[:, sl * TT:(sl + 1) * TT], xa, inc=vS)
            s.activation(SQ[:, sl * TT:(sl + 1) * TT], xa, AF.Square, inc=aS)
            t.wait(vS)
            t.wait(aS)
            if c == 0:
                ctx.bank_acquire([0, 1])
            t.matmul(bsum, C["ones128_bf"], XB[:, sl * TT:(sl + 1) * TT],
                     start=(c == 0), stop=(c == NCH - 1))
            t.matmul(bsq, C["ones128_bf"], SQ[:, sl * TT:(sl + 1) * TT],
                     start=(c == 0), stop=(c == NCH - 1),
                     inc=(mS if True else None))
        v.wait(mS)
        v.tensor_scalar_mul(mu, bsum, 1.0 / D, inc=vS)
        v.tensor_scalar_mul(ex2, bsq, 1.0 / D, inc=vS)
        ctx.bank_release([0, 1], vS)
        v.tensor_mul(msq, mu, mu)
        v.tensor_sub(ex2, ex2, msq)
        v.tensor_scalar_add(ex2, ex2, LN_EPS, inc=vS)
        s.wait(vS)
        s.activation(std, ex2, AF.Sqrt, inc=aS)
        v.wait(aS)
        v.reciprocal(rstd, std, inc=vS)
        v.wait(vS)
        v.scalar_tensor_tensor(nmu, mu, -1.0, rstd, ALU.mult, ALU.mult, inc=vS)
        t.wait(vS)
        ctx.bank_acquire([2, 3])
        t.matmul(brb, C["ones1x128_f32"], rstd, start=True, stop=True)
        t.matmul(bmb, C["ones1x128_f32"], nmu, start=True, stop=True, inc=mS)
        v.wait(mS)
        for c in range(NCH):
            xa = XRES[:, c * T + tt * TT: c * T + (tt + 1) * TT]
            xo = XN[:, c * T + tt * TT: c * T + (tt + 1) * TT]
            v.tensor_mul(TMP[:], xa, brb, inc=vS)
            v.tensor_add(xo, TMP[:], bmb, inc=vS)
        ctx.bank_release([2, 3], vS)
    ctx.scratch_release(mark)
    ctx.barrier(("tensor", "vector", "scalar"))


def emit_proj(ctx, name, W_dram, KK, M, rhs_at, evac, ntt=NTT, ttw=TT,
              wdtype=BF16, bank_base=0, nbanks=8):
    t, g = ctx.t, ctx.g
    qe = ctx.wq if W_dram.tensor.dtype == BF16 else ctx.g
    nK = KK // 128
    assert KK % 128 == 0
    assert nK * 128 * mybir.dt.size(wdtype) <= 8192, "weight strip too big"
    nOC = (M + 127) // 128
    mws = [min(128, M - oc * 128) for oc in range(nOC)]
    bufs = [ctx.WSLOT[sl][:].bitcast(wdtype) for sl in range(2)]
    ld = ([ctx.sem("prj_ly0"), ctx.sem("prj_ly1")]
          if W_dram.tensor.dtype == BF16
          else [ctx.sem("prj_ld0"), ctx.sem("prj_ld1")])
    mmS = ctx.sem("prj_mm")
    evS = ctx.sem("prj_ev")
    nb = min(ntt, nbanks // 2)
    ld_tgt = {}
    ev_after = {}
    for oc in range(nOC):
        sl = oc % 2
        if oc >= 2:
            qe.wait(evS, ev_after[oc - 2])
        src = W_dram[:, oc * 128:oc * 128 + mws[oc]].rearrange(
            "(k p) m -> p k m", p=128)
        d3 = bufs[sl][:, 0:nK * mws[oc]].rearrange("p (k m) -> p k m", k=nK)
        qe.dma_start(out=d3, in_=src, inc=ld[sl], inc_by=16)
        ld_tgt[oc] = ld[sl].n
        if oc >= 1:
            _proj_compute(ctx, oc - 1, sl ^ 1, mws, bufs, nK, ntt, nb, ttw,
                          bank_base, rhs_at, evac, ld, ld_tgt, mmS, evS, ev_after)
    for oc in (nOC - 2, nOC - 1):
        if oc >= 0 and oc not in ev_after:
            _proj_compute(ctx, oc, oc % 2, mws, bufs, nK, ntt, nb, ttw,
                          bank_base, rhs_at, evac, ld, ld_tgt, mmS, evS, ev_after)
    return evS


def _proj_compute(ctx, oc, sl, mws, bufs, nK, ntt, nb, ttw, bank_base,
                  rhs_at, evac, ld, ld_tgt, mmS, evS, ev_after):
    t = ctx.t
    mw = mws[oc]
    t.wait(ld[sl], ld_tgt[oc])
    for tt in range(ntt):
        bi = bank_base + sl * nb + (tt % nb)
        bank = ctx.banks[bi]
        ctx.bank_acquire([bi])
        ps = bank[0:mw, 0:ttw]
        for k in range(nK):
            t.matmul(ps, bufs[sl][:, k * mw:(k + 1) * mw], rhs_at(k, tt),
                     start=(k == 0), stop=(k == nK - 1),
                     inc=(mmS if k == nK - 1 else None))
        evac(oc, tt, ps, mmS, evS)
        ctx.bank_release([bi], evS)
    ev_after[oc] = evS.n


def emit_combine(ctx, C, WO, BO, OSB, GG, XRES, extra_sem=None):
    v, s = ctx.v, ctx.s
    mark = ctx.scratch_mark()
    BOS = ctx.scratch("bo", [128, NCH], F32)
    TMP = ctx.scratch("cmb_t", [128, TT], F32)
    TMP2 = ctx.scratch("cmb_u", [128, TT], F32)
    ldb = ctx.sem("cmb_ld")
    ctx.g.dma_start(out=BOS[:], in_=BO, inc=ldb, inc_by=16)

    def rhs_at(k, tt):
        return OSB[:, k * T + tt * TT: k * T + (tt + 1) * TT]

    def evac2(oc, tt, ps, mmS, evS):
        s.wait(mmS)
        s.wait(ldb, 16)
        if evS.n >= 2:
            s.wait(evS)
        s.activation(TMP[:], ps, AF.Identity, bias=BOS[:, oc:oc + 1], inc=evS)
        v.wait(evS)
        gg = GG[:, oc * T + tt * TT: oc * T + (tt + 1) * TT]
        xa = XRES[:, oc * T + tt * TT: oc * T + (tt + 1) * TT]
        v.tensor_mul(TMP2[:], TMP[:], gg)
        v.tensor_add(xa, xa, TMP2[:], inc=evS)

    evS = emit_proj(ctx, "wo", WO, D, D, rhs_at, evac2)
    ctx.scratch_release(mark)
    return evS


def _phase_nc(specs):
    nc = bass.Bass("TRN2", target_bir_lowering=False,
                   detect_race_conditions=False)
    dr = {n: nc.dram_tensor(n, list(sh), dt_, kind=kd).ap()
          for n, sh, dt_, kd in specs}
    ctx = Ctx(nc)
    return nc, ctx, dr


def setup_common(ctx, with_xn=True, with_gg=True):
    XN = ctx.fixed("XN", [128, NCH * T], BF16) if with_xn else None
    GG = ctx.fixed("GG", [128, NCH * T], BF16) if with_gg else None
    onesb = ctx.fixed("ones128b", [128, 1], BF16)
    onesf = ctx.fixed("ones128f", [128, 1], F32)
    ones1f = ctx.fixed("ones1x128f", [1, 128], F32)
    ones1b = ctx.fixed("ones1x128b", [1, 128], BF16)
    ctx.end_fixed()
    ctx.g.memset(onesb[:], 1.0)
    ctx.g.memset(onesf[:], 1.0)
    ctx.g.memset(ones1f[:], 1.0)
    ctx.g.memset(ones1b[:], 1.0)
    C = {"ones128_bf": onesb[:], "ones128_f32": onesf[:],
         "ones1x128_f32": ones1f[:], "ones1x128_bf": ones1b[:],
         "XN": XN, "GG": GG}
    return C


def build_ln_phase():
    specs = [("x", [D, T], F32, "ExternalInput"),
             ("xn", [D, T], BF16, "ExternalOutput")]
    nc, ctx, dr = _phase_nc(specs)
    C = setup_common(ctx)
    XRES = ctx.scratch("XRES", [128, NCH * T], F32)
    ctx._scr0 = ctx._scr
    emit_load_fm(ctx, XRES, dr["x"])
    for e in ("tensor", "vector", "scalar"):
        ctx.eng[e].wait(ctx.sdma)
    ctx.barrier(("gpsimd", "vector", "scalar", "tensor"))
    emit_ln(ctx, C, XRES, C["XN"])
    ctx.barrier(ENGS)
    emit_store_fm(ctx, dr["xn"], C["XN"])
    ctx.y.wait(ctx.sdma)
    return ctx.finalize()


def emit_load_fm(ctx, dst, dram_ap, chunks=NCH, cols=T, queue="y", dst_off=0):
    src = dram_ap.rearrange("(c p) t -> p c t", p=128)
    d3 = dst[:, dst_off:dst_off + chunks * cols].rearrange("p (c t) -> p c t", c=chunks)
    q = ctx.y if queue == "y" else ctx.g
    sem = ctx.sdma if queue == "y" else ctx.gdma
    q.dma_start(out=d3, in_=src, inc=sem, inc_by=16)


def emit_store_fm(ctx, dram_ap, srct, chunks=NCH, cols=T, queue="y"):
    dst = dram_ap.rearrange("(c p) t -> p c t", p=128)
    s3 = srct[:, 0:chunks * cols].rearrange("p (c t) -> p c t", c=chunks)
    q = ctx.y if queue == "y" else ctx.g
    sem = ctx.sdma if queue == "y" else ctx.gdma
    q.dma_start(out=dst, in_=s3, inc=sem, inc_by=16)


def emit_attn_core(ctx, C, XN, PXN, OSB, wqkv_ap, dhead, nheads, scale,
                   kcols_off, vcols_off, qbias=None, ngroups_limit=None, dbg=None):
    t, v, s, g = ctx.t, ctx.v, ctx.s, ctx.g
    assert dhead == 32
    ngroups = nheads * dhead // 128
    if ngroups_limit is not None:
        ngroups = ngroups_limit
    mark0 = ctx.scratch_mark()
    QF = [ctx.scratch(f"qf{i}", [128, T], BF16) for i in range(2)]
    KF = [ctx.scratch(f"kf{i}", [128, S], BF16) for i in range(2)]
    VT = ctx.scratch("vt", [128, NKC * 4 * (dhead + 1)], BF16)
    ctx._dbg_qf, ctx._dbg_kf, ctx._dbg_vt = QF[0], KF[0], VT
    if dbg:
        for x_ in (QF[0], QF[1], KF[0], KF[1], VT):
            ctx.g.memset(x_[:], 0.0)
    PRB = ctx.scratch("prb", [128, 4 * TT], BF16)
    WVB = [ctx.scratch(f"wvb{i}", [128, NCH * 128], BF16) for i in range(2)]
    KMX = ctx.scratch("kmx", [32, 8], F32)
    KMB = ctx.scratch("kmb", [32, 8], BF16)
    QAB = ctx.scratch("qab", [32, T], BF16)
    BRW = ctx.scratch("brw", [1, T], BF16)
    TO1 = ctx.scratch("to1", [32, TT], F32)
    RCF = ctx.scratch("rcf", [1, TT], F32)
    RCB = ctx.scratch("rcb", [1, TT], BF16)
    vS = ctx.sem("a3v"); aS = ctx.sem("a3a"); mS = ctx.sem("a3m")
    gS = ctx.sem("a3g"); wv_ld = [ctx.sem("a3wv0"), ctx.sem("a3wv1")]
    wv_done = {}

    def qk_rhs(k, tt):
        if tt < NTT:
            return XN[:, k * T + tt * TT: k * T + (tt + 1) * TT]
        return PXN[:, k * T + (tt - NTT) * TT: k * T + (tt - NTT + 1) * TT]

    def evac_pair(dsts):
        def evac(oc, tt, ps, mmS, evS):
            v.wait(mmS)
            A, B = dsts
            col = tt * TT
            v.tensor_copy(A[0:32, col:col + TT], ps[0:32, :], inc=evS)
            v.tensor_copy(A[64:96, col:col + TT], ps[32:64, :], inc=evS)
            v.tensor_copy(B[0:32, col:col + TT], ps[64:96, :], inc=evS)
            v.tensor_copy(B[64:96, col:col + TT], ps[96:128, :], inc=evS)
        return evac

    for grp in range(ngroups):
        emit_proj(ctx, f"q{grp}", wqkv_ap[:, grp * 128:(grp + 1) * 128], D, 128,
                  lambda k, tt: qk_rhs(k, tt), evac_pair((QF[0], QF[1])), ntt=NTT)
        ctx.barrier(("tensor", "vector", "gpsimd"))
        emit_proj(ctx, f"k{grp}",
                  wqkv_ap[:, kcols_off + grp * 128: kcols_off + (grp + 1) * 128],
                  D, 128, qk_rhs, evac_pair((KF[0], KF[1])), ntt=2 * NTT)
        ctx.barrier(("tensor", "vector", "gpsimd"))
        for KFx in KF:
            g.memset(KFx[32:34, :], 0.0)
            g.memset(KFx[32:33, :], 1.0)
            g.memset(KFx[96:98, :], 0.0)
            g.memset(KFx[96:97, :], 1.0)
        for QFx in QF:
            g.memset(QFx[32:34, :], 0.0)
            g.memset(QFx[96:98, :], 0.0, inc=gS)
        wv = wqkv_ap[:, vcols_off + grp * 128: vcols_off + (grp + 1) * 128]
        sl = grp % 2
        if grp >= 2:
            g.wait(vS, wv_done[grp - 2])
        g.dma_start(out=WVB[sl][:].rearrange("p (k m) -> p k m", k=NCH),
                    in_=wv.rearrange("(k p) m -> p k m", p=128),
                    inc=wv_ld[sl], inc_by=16)
        g.memset(VT[:], 1.0, inc=gS)
        t.wait(wv_ld[sl])
        t.wait(gS)
        VT4 = VT[:].rearrange("p (kc h j) -> p kc h j", kc=NKC, h=4)
        for kc in range(NKC):
            bank = ctx.banks[kc % 2]
            ctx.bank_acquire([kc % 2])
            ps = bank[:, 0:128]
            for k in range(NCH):
                if kc < 16:
                    lhsT = XN[:, k * T + kc * 128: k * T + (kc + 1) * 128]
                else:
                    lhsT = PXN[:, k * T + (kc - 16) * 128: k * T + (kc - 15) * 128]
                t.matmul(ps, lhsT, WVB[sl][:, k * 128:(k + 1) * 128],
                         start=(k == 0), stop=(k == NCH - 1),
                         inc=(mS if k == NCH - 1 else None))
            v.wait(mS)
            src3 = ps.rearrange("p (h j) -> p h j", h=4)
            v.tensor_copy(VT4[:, kc, :, 0:32], src3, inc=vS)
            ctx.bank_release([kc % 2], vS)
        wv_done[grp] = vS.n
        ctx.barrier(("tensor", "vector", "scalar", "gpsimd"))
        for hh in range(4):
            KFx, QFx = KF[hh // 2], QF[hh // 2]
            base = 64 * (hh % 2)
            v.reduce_max(KMX[:, hh:hh + 1], KFx[base:base + 32, :], axis=AX.X,
                         apply_absolute_value=True, inc=vS)
            v.wait(vS)
            v.tensor_copy(KMB[:, hh:hh + 1], KMX[:, hh:hh + 1], inc=vS)
            s.activation(QAB[:], QFx[base:base + 32, 0:T], AF.Abs, inc=aS)
            v.wait(ctx.gdma)
            t.wait(vS); t.wait(aS)
            for tt in range(NTT):
                bps = ctx.banks[4][0:1, :]
                ctx.bank_acquire([4])
                t.matmul(bps, KMB[:, hh:hh + 1], QAB[0:32, tt * TT:(tt + 1) * TT],
                         start=True, stop=True, inc=mS)
                v.wait(mS)
                v.tensor_scalar_mul(BRW[:, tt * TT:(tt + 1) * TT], bps, -1.0, inc=vS)
                ctx.bank_release([4], vS)
            g.wait(vS)
            g.dma_start(out=QFx[base + 32:base + 33, 0:T], in_=BRW[:],
                        inc=ctx.gdma, inc_by=16)
            t.wait(ctx.gdma)
            og = ctx.PG[1]
            sg = ctx.PG[0]
            for kc in range(NKC):
                ctx.bank_acquire([0, 1, 2, 3])
                if kc == 0:
                    ctx.bank_acquire([4, 5, 6, 7])
                for tt in range(NTT):
                    t.matmul(sg[:, tt * TT:(tt + 1) * TT],
                             KFx[base:base + 34, kc * 128:(kc + 1) * 128],
                             QFx[base:base + 34, tt * TT:(tt + 1) * TT],
                             start=True, stop=True,
                             inc=(mS if tt == NTT - 1 else None))
                s.wait(mS)
                s.activation(PRB[:], sg[:], AF.Exp, scale=scale, inc=aS)
                ctx.bank_release([0, 1, 2, 3], aS)
                t.wait(aS)
                for tt in range(NTT):
                    t.matmul(og[0:33, tt * TT:(tt + 1) * TT],
                             VT[:, kc * 132 + hh * 33: kc * 132 + hh * 33 + 33],
                             PRB[:, tt * TT:(tt + 1) * TT],
                             start=(kc == 0), stop=(kc == NKC - 1),
                             inc=(mS if (kc == NKC - 1 and tt == NTT - 1) else None))
            hglob = grp * 4 + hh
            c = hglob // 4
            rbase = (hglob % 4) * 32
            v.wait(mS)
            for tt in range(NTT):
                v.reciprocal(RCF[:], og[32:33, tt * TT:(tt + 1) * TT], inc=vS)
                v.wait(vS)
                v.tensor_copy(RCB[:], RCF[:], inc=vS)
                t.wait(vS)
                rb = sg[0:32, tt * TT:(tt + 1) * TT]
                ctx.bank_acquire([tt])
                t.matmul(rb, C["ones1x128_bf"][0:1, 0:32], RCB[:],
                         start=True, stop=True, inc=mS)
                v.wait(mS)
                v.tensor_copy(TO1[:], og[0:32, tt * TT:(tt + 1) * TT], inc=vS)
                v.tensor_mul(OSB[rbase:rbase + 32,
                                 c * T + tt * TT: c * T + (tt + 1) * TT],
                             TO1[:], rb, inc=vS)
                ctx.bank_release([tt], vS)
                ctx.bank_release([4 + tt], vS)
        ctx.barrier(("tensor", "vector", "scalar", "gpsimd"))
    ctx.scratch_release(mark0)


def build_attn3_phase(dbg=None, ngroups_limit=None, dump_srw=False):
    specs = [("x", [D, T], F32, "ExternalInput"),
             ("xn", [D, T], BF16, "ExternalInput"),
             ("pxn", [D, T], BF16, "ExternalInput"),
             ("gg", [D, T], BF16, "ExternalInput"),
             ("wqkv", [D, 3 * D], F32, "ExternalInput"),
             ("wout", [D, D], F32, "ExternalInput"),
             ("bout", [128, NCH], F32, "ExternalInput"),
             ("lng", [128, NCH], F32, "ExternalInput"),
             ("lnb", [128, NCH], F32, "ExternalInput"),
             ("xo", [D, T], F32, "ExternalOutput")]
    nc, ctx, dr = _phase_nc(specs)
    C = setup_common(ctx)
    ctx.constcol(1.0 / math.sqrt(32.0))
    OSB = ctx.scratch("OSB", [128, NCH * T], BF16)
    ctx._scr0 = ctx._scr
    mark_pxn = ctx.scratch_mark()
    PXN = ctx.scratch("PXN", [128, NCH * T], BF16)
    emit_load_fm(ctx, C["XN"], dr["xn"])
    emit_load_fm(ctx, PXN, dr["pxn"])
    emit_load_fm(ctx, C["GG"], dr["gg"])
    for e in ENGS:
        ctx.eng[e].wait(ctx.sdma)
    ctx.barrier(ENGS)
    if dbg:
        ctx.g.memset(OSB[:], 0.0)
    if dbg != "noattn":
        emit_attn_core(ctx, C, C["XN"], PXN, OSB, dr["wqkv"], 32, 32,
                       1.0 / math.sqrt(32.0), D, 2 * D,
                       ngroups_limit=ngroups_limit, dbg=dbg)
    ctx.barrier(ENGS)
    ctx.scratch_release(mark_pxn)
    if dbg in ("osb", "noattn"):
        osbo = nc.dram_tensor("osbo", [D, T], BF16, kind="ExternalOutput").ap()
        emit_store_fm(ctx, osbo, OSB)
        if dbg == "osb":
            qfo = nc.dram_tensor("qfo", [128, T], BF16, kind="ExternalOutput").ap()
            kfo = nc.dram_tensor("kfo", [128, S], BF16, kind="ExternalOutput").ap()
            vto = nc.dram_tensor("vto", [128, 4224], BF16, kind="ExternalOutput").ap()
            ctx.y.dma_start(out=qfo, in_=ctx._dbg_qf[:], inc=ctx.sdma, inc_by=16)
            ctx.y.dma_start(out=kfo, in_=ctx._dbg_kf[:], inc=ctx.sdma, inc_by=16)
            ctx.y.dma_start(out=vto, in_=ctx._dbg_vt[:], inc=ctx.sdma, inc_by=16)
        ctx.y.wait(ctx.sdma)
        return ctx.finalize()
    emit_l3_combine(ctx, C, dr, OSB)
    ctx.barrier(ENGS)
    emit_store_fm(ctx, dr["xo"], ctx._xres_out)
    if dump_srw:
        srwo = nc.dram_tensor("srwo", [D, T], BF16, kind="ExternalOutput").ap()
        emit_store_fm(ctx, srwo, C["XN"])
    ctx.y.wait(ctx.sdma)
    return ctx.finalize()


def emit_l3_combine(ctx, C, dr, OSB):
    t, v, s, g = ctx.t, ctx.v, ctx.s, ctx.g
    XRES = ctx.scratch("XRES", [128, NCH * T], F32)
    SRW = C["XN"]
    ctx._xres_out = XRES
    BO = ctx.scratch("bo3", [128, NCH], F32)
    LG = ctx.scratch("lg3", [128, NCH], F32)
    LB = ctx.scratch("lb3", [128, NCH], F32)
    SQ = ctx.scratch("sq3", [128, TT], BF16)
    TMP = ctx.scratch("tmp3", [128, TT], F32)
    TMP2 = ctx.scratch("tmp4", [128, TT], F32)
    ldS = ctx.sem("l3c_ld")
    emit_load_fm(ctx, XRES, dr["x"])
    g.dma_start(out=BO[:], in_=dr["bout"], inc=ldS, inc_by=16)
    g.dma_start(out=LG[:], in_=dr["lng"], inc=ldS, inc_by=16)
    g.dma_start(out=LB[:], in_=dr["lnb"], inc=ldS, inc_by=16)
    vS = ctx.sem("l3cv"); aS = ctx.sem("l3ca"); mS = ctx.sem("l3cm")

    def evac(oc, tt, ps, mmS, evS):
        s.wait(mmS)
        s.wait(ldS, 16)
        s.activation(SRW[:, oc * T + tt * TT: oc * T + (tt + 1) * TT], ps,
                     AF.Identity, bias=BO[:, oc:oc + 1], inc=evS)

    emit_proj(ctx, "wo3", dr["wout"], D, D,
              lambda k, tt: OSB[:, k * T + tt * TT: k * T + (tt + 1) * TT], evac)
    ctx.barrier(("tensor", "vector", "scalar", "gpsimd"))
    v.wait(ctx.sdma)
    s.wait(ldS, 48)
    mu = ctx.statrow(); ex2 = ctx.statrow(); msq = ctx.statrow()
    std = ctx.statrow(); rstd = ctx.statrow(); nmu = ctx.statrow()
    bsum, bsq = ctx.banks[0][0:1, :], ctx.banks[1][0:1, :]
    brb, bmb = ctx.banks[2], ctx.banks[3]
    for tt in range(NTT):
        for c in range(NCH):
            xa = SRW[:, c * T + tt * TT: c * T + (tt + 1) * TT]
            if c >= 1 or tt > 0:
                s.wait(mS)
            s.activation(SQ[:], xa, AF.Square, inc=aS)
            t.wait(aS)
            t.matmul(bsum, C["ones128_bf"], xa, start=(c == 0), stop=(c == NCH - 1))
            t.matmul(bsq, C["ones128_bf"], SQ[:], start=(c == 0),
                     stop=(c == NCH - 1), inc=mS)
        v.wait(mS)
        v.tensor_scalar_mul(mu, bsum, 1.0 / D)
        v.tensor_scalar_mul(ex2, bsq, 1.0 / D)
        v.tensor_mul(msq, mu, mu)
        v.tensor_sub(ex2, ex2, msq)
        v.tensor_scalar_add(ex2, ex2, 1e-6, inc=vS)
        s.wait(vS)
        s.activation(std, ex2, AF.Sqrt, inc=aS)
        v.wait(aS)
        v.reciprocal(rstd, std, inc=vS)
        v.wait(vS)
        v.scalar_tensor_tensor(nmu, mu, -1.0, rstd, ALU.mult, ALU.mult, inc=vS)
        t.wait(vS)
        t.matmul(brb, C["ones1x128_f32"], rstd, start=True, stop=True)
        t.matmul(bmb, C["ones1x128_f32"], nmu, start=True, stop=True, inc=mS)
        v.wait(mS)
        for c in range(NCH):
            col = c * T + tt * TT
            sr = SRW[:, col:col + TT]
            v.tensor_copy(TMP[:], sr)
            v.tensor_mul(TMP2[:], TMP[:], brb)
            v.tensor_add(TMP[:], TMP2[:], bmb, inc=vS)
            s.wait(vS)
            s.activation(TMP2[:], TMP[:], AF.Identity, bias=LB[:, c:c + 1],
                         scale=LG[:, c:c + 1], inc=aS)
            v.wait(aS)
            v.tensor_mul(TMP[:], TMP2[:], C["GG"][:, col:col + TT])
            v.tensor_add(XRES[:, col:col + TT], XRES[:, col:col + TT], TMP[:],
                         inc=vS)
        t.wait(vS)



def emit_ffn3(ctx, C, dr, hsp, wpq):
    t, v, s, g = ctx.t, ctx.v, ctx.s, ctx.g
    XN, GG = C["XN"], C["GG"]
    mark = ctx.scratch_mark()
    WS = [ctx.WSLOT[i][:].bitcast(F32) for i in range(2)]
    QS = [ctx.scratch(f"f3q{i}", [128, 1024], BF16) for i in range(2)]
    QA = ctx.scratch("f3qa", [128, 1024], BF16)
    QB = ctx.scratch("f3qb", [128, 1024], BF16)
    ACC = ctx.scratch("f3acc", [128, 2], F32)
    PRT = ctx.scratch("f3prt", [128, 2], F32)
    THRC = ctx.scratch("f3thr", [128, 4], F32)
    GR = ctx.statrow(8)
    HB = [ctx.scratch(f"f3hb{i}", [128, T], BF16) for i in range(2)]
    SQB = ctx.scratch("f3sqb", [128, TT], BF16)
    SUMS = ctx.scratch("f3sums", [1, T], F32)
    SQS = ctx.scratch("f3sqs", [1, T], F32)
    MUB = ctx.scratch("f3mub", [128, T], F32)
    RSB = ctx.scratch("f3rsb", [128, T], F32)
    HKT = [ctx.scratch(f"f3hkt{i}", [128, TT], BF16) for i in range(2)]
    WPS = [ctx.scratch(f"f3wp{i}", [128, 1024], BF16) for i in range(2)]
    HNK = ctx.scratch("f3hnk", [128, TT], BF16)
    TMF = ctx.scratch("f3tmf", [128, TT], F32)
    TMF2 = ctx.scratch("f3tmf2", [128, TT], F32)
    LNW = ctx.scratch("f3lnw", [128, 32], F32)
    LNB = ctx.scratch("f3lnb", [128, 32], F32)
    vS, aS, mS = ctx.sem("f3v"), ctx.sem("f3a"), ctx.sem("f3m")
    ldS = ctx.sem("f3ld")
    ctx._f3thr, ctx._f3sums, ctx._f3sqs = THRC, SUMS, SQS
    zS = ctx.sem("f3z")
    g.dma_start(out=LNW[:], in_=dr["lnw"], inc=ldS, inc_by=16)
    g.dma_start(out=LNB[:], in_=dr["lnb"], inc=ldS, inc_by=16)
    g.memset(ACC[:], 0.0)
    g.memset(SUMS[:], 0.0)
    g.memset(SQS[:], 0.0, inc=zS)

    slot_free = [0, 0]
    ld_tgt = [0, 0]
    v.wait(zS)
    for wi, wd in enumerate((dr["wfc"], dr["wproj"])):
        KK, M = wd.shape
        nhalf = (M + 2047) // 2048
        cw = M // nhalf
        for i in range(KK // 128 * nhalf):
            sl = i % 2
            kc, half = divmod(i, nhalf)
            if slot_free[sl] or ld_tgt[sl]:
                g.wait(vS, slot_free[sl])
            g.dma_start(out=WS[sl][:, 0:cw],
                        in_=wd[kc * 128:(kc + 1) * 128, half * cw:(half + 1) * cw],
                        inc=ldS, inc_by=16)
            ld_tgt[sl] = ldS.n
            v.wait(ldS, ld_tgt[sl])
            v.reduce_sum(PRT[:, sl:sl + 1], WS[sl][:, 0:cw], axis=AX.X,
                         apply_absolute_value=True, inc=vS)
            v.wait(vS)
            v.tensor_add(ACC[:, wi:wi + 1], ACC[:, wi:wi + 1], PRT[:, sl:sl + 1],
                         inc=vS)
            slot_free[sl] = vS.n
    for wi, wd in enumerate((dr["wfc"], dr["wproj"])):
        numel = float(wd.shape[0] * wd.shape[1])
        ctx.bank_acquire([0])
        t.wait(vS)
        t.matmul(ctx.banks[0][0:1, 0:1], C["ones128_f32"], ACC[:, wi:wi + 1],
                 start=True, stop=True, inc=mS)
        v.wait(mS)
        v.tensor_scalar(GR[0:1, 2 * wi:2 * wi + 1], ctx.banks[0][0:1, 0:1],
                        0.5 / numel, 0.5e-6, ALU.mult, ALU.add, inc=vS)
        v.wait(vS)
        v.tensor_scalar_mul(GR[0:1, 2 * wi + 1:2 * wi + 2],
                            GR[0:1, 2 * wi:2 * wi + 1], -1.0, inc=vS)
        ctx.bank_release([0], vS)
    ctx.bank_acquire([1])
    t.wait(vS)
    t.matmul(ctx.banks[1][0:128, 0:4], C["ones1x128_f32"], GR[0:1, 0:4],
             start=True, stop=True, inc=mS)
    v.wait(mS)
    v.tensor_copy(THRC[:], ctx.banks[1][0:128, 0:4], inc=vS)
    ctx.bank_release([1], vS)

    qfree = [0, 0]
    for kc in range(32):
        sl = kc % 2
        g.wait(vS, slot_free[sl])
        g.dma_start(out=WS[sl][:, 0:1024],
                    in_=dr["wproj"][kc * 128:(kc + 1) * 128, :], inc=ldS, inc_by=16)
        ld_tgt[sl] = ldS.n
        v.wait(ldS, ld_tgt[sl])
        if qfree[sl]:
            v.wait(ctx.gdma, qfree[sl])
        v.tensor_single_scalar(QA[:], WS[sl][:, 0:1024], THRC[:, 2:3], ALU.is_gt)
        v.tensor_single_scalar(QB[:], WS[sl][:, 0:1024], THRC[:, 3:4], ALU.is_lt,
                               inc=vS)
        slot_free[sl] = vS.n
        v.tensor_sub(QS[sl][:], QA[:], QB[:], inc=vS)
        g.wait(vS)
        g.dma_start(out=wpq[kc * 128:(kc + 1) * 128, :], in_=QS[sl][:],
                    inc=ctx.gdma, inc_by=16)
        qfree[sl] = ctx.gdma.n

    hb_free = [0, 0]
    qspill = list(qfree)
    qfree_b1 = [0, 0]
    for oc in range(32):
        sl = oc % 2
        g.wait(vS, slot_free[sl])
        g.dma_start(out=WS[sl][:, 0:1024].rearrange("p (k m) -> p k m", k=8),
                    in_=dr["wfc"][:, oc * 128:(oc + 1) * 128].rearrange(
                        "(k p) m -> p k m", p=128),
                    inc=ldS, inc_by=16)
        ld_tgt[sl] = ldS.n
        v.wait(ldS, ld_tgt[sl])
        if qspill[sl]:
            v.wait(ctx.gdma, qspill[sl])
            qspill[sl] = 0
        if qfree_b1[sl]:
            v.wait(aS, qfree_b1[sl])
        v.tensor_single_scalar(QA[:], WS[sl][:, 0:1024], THRC[:, 0:1], ALU.is_gt)
        v.tensor_single_scalar(QB[:], WS[sl][:, 0:1024], THRC[:, 1:2], ALU.is_lt,
                               inc=vS)
        slot_free[sl] = vS.n
        v.tensor_sub(QS[sl][:], QA[:], QB[:], inc=vS)
        t.wait(vS)
        if hb_free[sl]:
            s.wait(ctx.gdma, hb_free[sl])
        for tt in range(NTT):
            bi = sl * 2 + tt % 2
            ctx.bank_acquire([bi])
            ps = ctx.banks[bi]
            for k in range(NCH):
                t.matmul(ps, QS[sl][:, k * 128:(k + 1) * 128],
                         XN[:, k * T + tt * TT: k * T + (tt + 1) * TT],
                         start=(k == 0), stop=(k == NCH - 1),
                         inc=(mS if k == NCH - 1 else None))
            s.wait(mS)
            hsl = HB[sl][:, tt * TT:(tt + 1) * TT]
            s.activation(hsl, ps, AF.Gelu, inc=aS)
            ctx.bank_release([bi], aS)
            s.wait(aS)
            s.activation(SQB[:], hsl, AF.Square, inc=aS)
            t.wait(aS)
            ctx.bank_acquire([4, 5])
            t.matmul(ctx.banks[4][0:1, :], C["ones128_bf"], hsl,
                     start=True, stop=True)
            t.matmul(ctx.banks[5][0:1, :], C["ones128_bf"], SQB[:],
                     start=True, stop=True, inc=mS)
            v.wait(mS)
            v.tensor_add(SUMS[0:1, tt * TT:(tt + 1) * TT],
                         SUMS[0:1, tt * TT:(tt + 1) * TT], ctx.banks[4][0:1, :],
                         inc=vS)
            v.tensor_add(SQS[0:1, tt * TT:(tt + 1) * TT],
                         SQS[0:1, tt * TT:(tt + 1) * TT], ctx.banks[5][0:1, :],
                         inc=vS)
            ctx.bank_release([4, 5], vS)
        qfree_b1[sl] = aS.n
        g.wait(aS)
        g.dma_start(out=hsp[oc * 128:(oc + 1) * 128, :], in_=HB[sl][:],
                    inc=ctx.gdma, inc_by=16)
        hb_free[sl] = ctx.gdma.n

    mu = ctx.statrow(); ex2 = ctx.statrow(); msq = ctx.statrow()
    stdr = ctx.statrow(); rstd = ctx.statrow(); nmu = ctx.statrow()
    for tt in range(NTT):
        sl_ = slice(tt * TT, (tt + 1) * TT)
        v.tensor_scalar_mul(mu, SUMS[0:1, sl_], 1.0 / 4096.0)
        v.tensor_scalar_mul(ex2, SQS[0:1, sl_], 1.0 / 4096.0)
        v.tensor_mul(msq, mu, mu)
        v.tensor_sub(ex2, ex2, msq)
        v.tensor_scalar_add(ex2, ex2, LN_EPS, inc=vS)
        s.wait(vS)
        s.activation(stdr, ex2, AF.Sqrt, inc=aS)
        v.wait(aS)
        v.reciprocal(rstd, stdr, inc=vS)
        v.wait(vS)
        v.scalar_tensor_tensor(nmu, mu, -1.0, rstd, ALU.mult, ALU.mult, inc=vS)
        t.wait(vS)
        ctx.bank_acquire([6, 7])
        t.matmul(ctx.banks[6][:, 0:TT], C["ones1x128_f32"], rstd,
                 start=True, stop=True)
        t.matmul(ctx.banks[7][:, 0:TT], C["ones1x128_f32"], nmu,
                 start=True, stop=True, inc=mS)
        v.wait(mS)
        v.tensor_copy(RSB[:, sl_], ctx.banks[6][:, 0:TT], inc=vS)
        v.tensor_copy(MUB[:, sl_], ctx.banks[7][:, 0:TT], inc=vS)
        ctx.bank_release([6, 7], vS)

    hk_free = [0, 0]
    wp_free = [0, 0]
    hk_tgt = [0, 0]
    wp_tgt = [0, 0]
    wpld = ctx.sem("f3wp")
    hkld = ctx.sem("f3hk")
    hnk_prev = 0
    for q in range(NTT):
        qs_ = slice(q * TT, (q + 1) * TT)
        for k in range(32):
            sl = k % 2
            if hk_free[sl]:
                g.wait(mS, hk_free[sl])
            g.dma_start(out=HKT[sl][:],
                        in_=hsp[k * 128:(k + 1) * 128, q * TT:(q + 1) * TT],
                        inc=hkld, inc_by=16)
            hk_tgt[sl] = hkld.n
            if wp_free[sl]:
                g.wait(mS, wp_free[sl])
            g.dma_start(out=WPS[sl][:], in_=wpq[k * 128:(k + 1) * 128, :],
                        inc=wpld, inc_by=16)
            wp_tgt[sl] = wpld.n
            v.wait(hkld, hk_tgt[sl])
            v.wait(aS)
            v.tensor_copy(TMF[:], HKT[sl][:])
            v.tensor_mul(TMF2[:], TMF[:], RSB[:, qs_])
            v.tensor_add(TMF[:], TMF2[:], MUB[:, qs_], inc=vS)
            s.wait(vS)
            if hnk_prev:
                s.wait(mS, hnk_prev)
            s.activation(HNK[:], TMF[:], AF.Identity, scale=LNW[:, k:k + 1],
                         bias=LNB[:, k:k + 1], inc=aS)
            t.wait(aS)
            t.wait(wpld, wp_tgt[sl])
            for oc in range(NCH):
                if k == 0:
                    ctx.bank_acquire([oc])
                t.matmul(ctx.banks[oc], WPS[sl][:, oc * 128:(oc + 1) * 128],
                         HNK[:], start=(k == 0), stop=(k == 31),
                         inc=(mS if oc == NCH - 1 else None))
            hk_free[sl] = mS.n
            wp_free[sl] = mS.n
            hnk_prev = mS.n
        for oc in range(NCH):
            s.wait(mS)
            s.activation(GG[:, oc * T + q * TT: oc * T + (q + 1) * TT],
                         ctx.banks[oc], AF.Silu, inc=aS)
            ctx.bank_release([oc], aS)
    ctx.scratch_release(mark)


def build_ffn3_phase(dbg=False):
    specs = [("xn", [D, T], BF16, "ExternalInput"),
             ("wfc", [D, 4096], F32, "ExternalInput"),
             ("wproj", [4096, D], F32, "ExternalInput"),
             ("lnw", [128, 32], F32, "ExternalInput"),
             ("lnb", [128, 32], F32, "ExternalInput"),
             ("gg", [D, T], BF16, "ExternalOutput")]
    nc, ctx, dr = _phase_nc(specs)
    C = setup_common(ctx)
    hsp = nc.dram_tensor("hsp", [4096, T], BF16, kind="Internal").ap()
    wpq = nc.dram_tensor("wpq", [4096, D], BF16, kind="Internal").ap()
    emit_load_fm(ctx, C["XN"], dr["xn"])
    for e in ENGS:
        ctx.eng[e].wait(ctx.sdma)
    ctx.barrier(ENGS)
    emit_ffn3(ctx, C, dr, hsp, wpq)
    ctx.barrier(ENGS)
    emit_store_fm(ctx, dr["gg"], C["GG"])
    if dbg:
        thro = nc.dram_tensor("thro", [128, 4], F32, kind="ExternalOutput").ap()
        h0o = nc.dram_tensor("h0o", [128, T], BF16, kind="ExternalOutput").ap()
        sumo = nc.dram_tensor("sumo", [1, T], F32, kind="ExternalOutput").ap()
        sqo = nc.dram_tensor("sqo", [1, T], F32, kind="ExternalOutput").ap()
        wq0 = nc.dram_tensor("wq0", [128, D], BF16, kind="ExternalOutput").ap()
        ctx.y.dma_start(out=thro, in_=ctx._f3thr[:], inc=ctx.sdma, inc_by=16)
        ctx.y.dma_start(out=sumo, in_=ctx._f3sums[:], inc=ctx.sdma, inc_by=16)
        ctx.y.dma_start(out=sqo, in_=ctx._f3sqs[:], inc=ctx.sdma, inc_by=16)
        ctx.y.dma_start(out=h0o, in_=hsp[0:128, :], inc=ctx.sdma, inc_by=16)
        ctx.y.dma_start(out=wq0, in_=wpq[0:128, :], inc=ctx.sdma, inc_by=16)
    ctx.y.wait(ctx.sdma)
    return ctx.finalize()

def emit_attn2(ctx, C, XN, PXN, OSB, dr, ngroups_limit=None, dbg=None):
    t, v, s, g = ctx.t, ctx.v, ctx.s, ctx.g
    NH, DH = 16, 64
    scale = 1.0 / 8.0
    mark0 = ctx.scratch_mark()
    QF = [ctx.scratch(f"2qf{i}", [128, T], BF16) for i in range(2)]
    KF = [ctx.scratch(f"2kf{i}", [128, S], BF16) for i in range(2)]
    VT = ctx.scratch("2vt", [128, NKC * 2 * (DH + 1)], BF16)
    PRB = ctx.scratch("2prb", [128, 4 * TT], BF16)
    WVB = [ctx.scratch(f"2wvb{i}", [128, NCH * 128], BF16) for i in range(2)]
    KMX = ctx.scratch("2kmx", [64, 4], F32)
    KMB = ctx.scratch("2kmb", [64, 4], BF16)
    QAB = ctx.scratch("2qab", [64, T], BF16)
    BRW = ctx.scratch("2brw", [1, T], BF16)
    TO1 = ctx.scratch("2to1", [64, TT], F32)
    RCF = ctx.scratch("2rcf", [1, TT], F32)
    RCB = ctx.scratch("2rcb", [1, TT], BF16)
    BQC = ctx.scratch("2bqc", [128, NCH], F32)
    vS, aS, mS = ctx.sem("2v"), ctx.sem("2a"), ctx.sem("2m")
    gS = ctx.sem("2g")
    ldS = ctx.sem("2ld")
    wv_ld = [ctx.sem("2wv0"), ctx.sem("2wv1")]
    wv_done = {}
    ONESB = ctx.scratch("2onesb", [1, T], BF16)
    ALIB = ctx.scratch("2alib", [128, 512], F32)
    g.memset(ONESB[:], 1.0)
    g.dma_start(out=BQC[:], in_=dr["bq"], inc=ldS, inc_by=16)
    g.dma_start(out=ALIB[:], in_=dr["alib"], inc=ldS, inc_by=16)
    ngroups = NH // 2
    if ngroups_limit is not None:
        ngroups = ngroups_limit

    def qk_rhs(k, tt):
        if tt < NTT:
            return XN[:, k * T + tt * TT: k * T + (tt + 1) * TT]
        return PXN[:, k * T + (tt - NTT) * TT: k * T + (tt - NTT + 1) * TT]

    def evac_k(oc, tt, ps, mmS, evS):
        v.wait(mmS)
        col = tt * TT
        v.tensor_copy(KF[0][0:64, col:col + TT], ps[0:64, :], inc=evS)
        v.tensor_copy(KF[1][0:64, col:col + TT], ps[64:128, :], inc=evS)

    for grp in range(ngroups):
        def evac_qg(oc, tt, ps, mmS, evS, grp=grp):
            s.wait(mmS)
            s.wait(ldS, 16)
            col = tt * TT
            s.activation(QF[0][0:64, col:col + TT], ps[0:64, :], AF.Identity,
                         bias=BQC[0:64, grp:grp + 1], inc=evS)
            s.activation(QF[1][0:64, col:col + TT], ps[64:128, :], AF.Identity,
                         bias=BQC[64:128, grp:grp + 1], inc=evS)
        emit_proj(ctx, f"2q{grp}", dr["wq"][:, grp * 128:(grp + 1) * 128], D, 128,
                  lambda k, tt: qk_rhs(k, tt), evac_qg, ntt=NTT)
        ctx.barrier(("tensor", "vector", "scalar", "gpsimd"))
        emit_proj(ctx, f"2k{grp}", dr["wk"][:, grp * 128:(grp + 1) * 128], D, 128,
                  qk_rhs, evac_k, ntt=2 * NTT)
        ctx.barrier(("tensor", "vector", "scalar", "gpsimd"))
        for i in range(2):
            hh = grp * 2 + i
            g.memset(KF[i][64:68, :], 0.0)
            g.memset(KF[i][64:65, :], 1.0)
            g.memset(QF[i][64:68, :], 0.0, inc=gS)
            g.wait(gS)
            g.dma_start(out=KF[i][65:66, :], in_=dr["pmask"], inc=ctx.gdma, inc_by=16)
            g.dma_start(out=KF[i][66:67, 0:T], in_=ONESB[0:1, 0:T],
                        inc=ctx.gdma, inc_by=16)
            g.dma_start(out=KF[i][66:67, T:S], in_=ONESB[0:1, 0:T],
                        inc=ctx.gdma, inc_by=16)
            g.dma_start(out=QF[i][65:66, 0:T], in_=ONESB[0:1, 0:T],
                        inc=ctx.gdma, inc_by=16)
            g.dma_start(out=QF[i][66:67, 0:T], in_=dr["qalibi"][hh:hh + 1, :],
                        inc=ctx.gdma, inc_by=16)
        sl = grp % 2
        if grp >= 2:
            g.wait(vS, wv_done[grp - 2])
        g.dma_start(out=WVB[sl][:].rearrange("p (k m) -> p k m", k=NCH),
                    in_=dr["wv"][:, grp * 128:(grp + 1) * 128].rearrange(
                        "(k p) m -> p k m", p=128),
                    inc=wv_ld[sl], inc_by=16)
        g.memset(VT[:], 1.0, inc=gS)
        t.wait(wv_ld[sl])
        t.wait(gS)
        VT4 = VT[:].rearrange("p (kc h j) -> p kc h j", kc=NKC, h=2)
        for kc in range(NKC):
            ctx.bank_acquire([kc % 2])
            ps = ctx.banks[kc % 2][:, 0:128]
            for k in range(NCH):
                if kc < 16:
                    lhsT = XN[:, k * T + kc * 128: k * T + (kc + 1) * 128]
                else:
                    lhsT = PXN[:, k * T + (kc - 16) * 128: k * T + (kc - 15) * 128]
                t.matmul(ps, lhsT, WVB[sl][:, k * 128:(k + 1) * 128],
                         start=(k == 0), stop=(k == NCH - 1),
                         inc=(mS if k == NCH - 1 else None))
            v.wait(mS)
            v.tensor_copy(VT4[:, kc, :, 0:64],
                          ps.rearrange("p (h j) -> p h j", h=2), inc=vS)
            ctx.bank_release([kc % 2], vS)
        wv_done[grp] = vS.n
        ctx.barrier(("tensor", "vector", "scalar", "gpsimd"))
        for i in range(2):
            hh = grp * 2 + i
            KFx, QFx = KF[i], QF[i]
            v.reduce_max(KMX[:, i:i + 1], KFx[0:64, :], axis=AX.X,
                         apply_absolute_value=True, inc=vS)
            v.wait(vS)
            v.tensor_copy(KMB[:, i:i + 1], KMX[:, i:i + 1], inc=vS)
            v.wait(ctx.gdma)
            s.activation(QAB[:], QFx[0:64, 0:T], AF.Abs, inc=aS)
            t.wait(vS); t.wait(aS)
            for tt in range(NTT):
                bps = ctx.banks[4][0:1, :]
                ctx.bank_acquire([4])
                t.matmul(bps, KMB[:, i:i + 1], QAB[0:64, tt * TT:(tt + 1) * TT],
                         start=True, stop=True, inc=mS)
                v.wait(mS)
                v.tensor_scalar_mul(BRW[:, tt * TT:(tt + 1) * TT], bps, -1.0,
                                    inc=vS)
                ctx.bank_release([4], vS)
            g.wait(vS)
            g.dma_start(out=QFx[64:65, 0:T], in_=BRW[:], inc=ctx.gdma, inc_by=16)
            t.wait(ctx.gdma)
            og = ctx.PG[1]
            sg = ctx.PG[0]
            nkc_own = 16
            first_pv = True
            for kc in range(NKC):
                own = kc < nkc_own
                tmin = 0
                if own:
                    tmin = max(0, -(-(kc - 3) // 4))
                if tmin >= NTT:
                    continue
                ctx.bank_acquire(list(range(tmin, 4)))
                if first_pv:
                    ctx.bank_acquire([4, 5, 6, 7])
                    first_pv = False
                for tt in range(tmin, NTT):
                    t.matmul(sg[:, tt * TT:(tt + 1) * TT],
                             KFx[0:67, kc * 128:(kc + 1) * 128],
                             QFx[0:67, tt * TT:(tt + 1) * TT],
                             start=True, stop=True,
                             inc=(mS if tt == NTT - 1 else None))
                s.wait(mS)
                s.wait(ldS, 32)
                s.activation(PRB[:, tmin * TT:4 * TT],
                             sg[:, tmin * TT:4 * TT], AF.Exp, scale=scale,
                             bias=ALIB[:, hh * 32 + kc: hh * 32 + kc + 1], inc=aS)
                need_mask = own and any(4 * tt <= kc <= 4 * tt + 3
                                        for tt in range(tmin, NTT))
                if need_mask:
                    g.wait(aS)
                    dts = [tt for tt in range(tmin, NTT) if 4 * tt <= kc <= 4 * tt + 3]
                    for j, tt in enumerate(dts):
                        g.affine_select(
                            out=PRB[:, tt * TT:(tt + 1) * TT],
                            in_=PRB[:, tt * TT:(tt + 1) * TT],
                            pattern=[[1, TT]], compare_op=ALU.is_ge,
                            fill=0.0, base=tt * TT - kc * 128,
                            channel_multiplier=-1,
                            inc=(gS if j == len(dts) - 1 else None))
                    t.wait(gS)
                else:
                    t.wait(aS)
                ctx.bank_release(list(range(tmin, 4)), aS if not need_mask else gS)
                for tt in range(tmin, NTT):
                    t.matmul(og[0:65, tt * TT:(tt + 1) * TT],
                             VT[:, kc * 130 + i * 65: kc * 130 + i * 65 + 65],
                             PRB[:, tt * TT:(tt + 1) * TT],
                             start=(kc == 0), stop=(kc == NKC - 1),
                             inc=(mS if (kc == NKC - 1 and tt == NTT - 1) else None))
            c = hh // 2
            rbase = (hh % 2) * 64
            v.wait(mS)
            for tt in range(NTT):
                v.reciprocal(RCF[:], og[64:65, tt * TT:(tt + 1) * TT], inc=vS)
                v.wait(vS)
                v.tensor_copy(RCB[:], RCF[:], inc=vS)
                t.wait(vS)
                rb = sg[0:64, tt * TT:(tt + 1) * TT]
                ctx.bank_acquire([tt])
                t.matmul(rb, C["ones1x128_bf"][0:1, 0:64], RCB[:],
                         start=True, stop=True, inc=mS)
                v.wait(mS)
                v.tensor_copy(TO1[:], og[0:64, tt * TT:(tt + 1) * TT], inc=vS)
                v.tensor_mul(OSB[rbase:rbase + 64,
                                 c * T + tt * TT: c * T + (tt + 1) * TT],
                             TO1[:], rb, inc=vS)
                ctx.bank_release([tt], vS)
                ctx.bank_release([4 + tt], vS)
        ctx.barrier(("tensor", "vector", "scalar", "gpsimd"))
    ctx.scratch_release(mark0)

def build_attn2_phase(dbg=None, ngroups_limit=None):
    specs = [("x", [D, T], F32, "ExternalInput"),
             ("xn", [D, T], BF16, "ExternalInput"),
             ("pxn", [D, T], BF16, "ExternalInput"),
             ("gg", [D, T], BF16, "ExternalInput"),
             ("wq", [D, D], F32, "ExternalInput"),
             ("bq", [128, NCH], F32, "ExternalInput"),
             ("wk", [D, D], F32, "ExternalInput"),
             ("wv", [D, D], F32, "ExternalInput"),
             ("wo", [D, D], F32, "ExternalInput"),
             ("bo", [128, NCH], F32, "ExternalInput"),
             ("alib", [128, 512], F32, "ExternalInput"),
             ("qalibi", [16, T], BF16, "ExternalInput"),
             ("pmask", [1, S], BF16, "ExternalInput"),
             ("xo", [D, T], F32, "ExternalOutput")]
    nc, ctx, dr = _phase_nc(specs)
    C = setup_common(ctx)
    OSB = ctx.scratch("OSB", [128, NCH * T], BF16)
    ctx._scr0 = ctx._scr
    mark_pxn = ctx.scratch_mark()
    PXN = ctx.scratch("PXN", [128, NCH * T], BF16)
    emit_load_fm(ctx, C["XN"], dr["xn"])
    emit_load_fm(ctx, PXN, dr["pxn"])
    emit_load_fm(ctx, C["GG"], dr["gg"])
    for e in ENGS:
        ctx.eng[e].wait(ctx.sdma)
    ctx.barrier(ENGS)
    if dbg:
        ctx.g.memset(OSB[:], 0.0)
    emit_attn2(ctx, C, C["XN"], PXN, OSB, dr, ngroups_limit=ngroups_limit, dbg=dbg)
    ctx.barrier(ENGS)
    ctx.scratch_release(mark_pxn)
    if dbg == "osb":
        osbo = nc.dram_tensor("osbo", [D, T], BF16, kind="ExternalOutput").ap()
        emit_store_fm(ctx, osbo, OSB)
        ctx.y.wait(ctx.sdma)
        return ctx.finalize()
    XRES = ctx.scratch("XRES", [128, NCH * T], F32)
    emit_load_fm(ctx, XRES, dr["x"])
    for e in ("vector", "scalar"):
        ctx.eng[e].wait(ctx.sdma)
    emit_combine(ctx, C, dr["wo"], dr["bo"], OSB, C["GG"], XRES)
    ctx.barrier(ENGS)
    emit_store_fm(ctx, dr["xo"], XRES)
    ctx.y.wait(ctx.sdma)
    return ctx.finalize()

def emit_ffn2(ctx, C, dr):
    t, v, s, g = ctx.t, ctx.v, ctx.s, ctx.g
    XN, GG = C["XN"], C["GG"]
    mark = ctx.scratch_mark()
    RWT = ctx.scratch("m_rwt", [128, NCH * 16], BF16)
    RBB = ctx.scratch("m_rbb", [1, 16], BF16)
    EYE = ctx.scratch("m_eye", [128, 128], BF16)
    ESEL = ctx.scratch("m_esel", [17, 17 * 128], BF16)
    ONE1 = ctx.scratch("m_one1", [1, 128], BF16)
    PRF = ctx.scratch("m_prf", [128, 16], F32)
    PRW = ctx.scratch("m_prw", [128, 16], F32)
    EQT = ctx.scratch("m_eqt", [128, 16], F32)
    MXC = ctx.scratch("m_mxc", [128, 4], F32)
    SDI = ctx.scratch("m_sdi", [128, 2], F32)
    GFB = ctx.scratch("m_gfb", [128, 17], BF16)
    GT = ctx.scratch("m_gt", [17, T], BF16)
    ctx._m_gt = GT
    ctx._m_dbg = (PRF, MXC, SDI, EQT)
    GEB = ctx.scratch("m_geb", [128, 16 * TT], BF16)
    XE = [ctx.scratch(f"m_xe{i}", [128, NCH * TT], BF16) for i in range(2)]
    ACCB = ctx.scratch("m_accb", [128, T], F32)
    TMPM = ctx.scratch("m_tmp", [128, TT], F32)
    vS, aS, mS, ldS, gS = (ctx.sem("m_v"), ctx.sem("m_a"), ctx.sem("m_m"),
                           ctx.sem("m_ld"), ctx.sem("m_g"))
    g.dma_start(out=RWT[:].rearrange("p (k m) -> p k m", k=NCH),
                in_=dr["rwt"].rearrange("(k p) m -> p k m", p=128),
                inc=ldS, inc_by=16)
    g.dma_start(out=RBB[:], in_=dr["rbb"], inc=ldS, inc_by=16)
    g.dma_start(out=EYE[:], in_=dr["eye"], inc=ldS, inc_by=16)
    g.dma_start(out=ESEL[:], in_=dr["esel"], inc=ldS, inc_by=16)
    g.memset(ONE1[:], 1.0, inc=gS)

    for tc in range(16):
        ctx.bank_acquire([0])
        t.wait(ldS, 64)
        t.wait(gS, gS.n)
        rp = ctx.banks[0][:, 0:16]
        for k in range(NCH):
            t.matmul(rp, XN[:, k * T + tc * 128: k * T + (tc + 1) * 128],
                     RWT[:, k * 16:(k + 1) * 16], start=(k == 0), stop=False)
        t.matmul(rp, ONE1[:], RBB[:], start=False, stop=True, inc=mS)
        v.wait(mS)
        v.reduce_max(MXC[:, 0:1], rp, axis=AX.X, inc=vS)
        v.wait(vS)
        v.tensor_scalar_mul(MXC[:, 1:2], MXC[:, 0:1], -1.0, inc=vS)
        s.wait(vS)
        s.activation(PRW[:], rp, AF.Exp, bias=MXC[:, 1:2], inc=aS)
        ctx.bank_release([0], aS)
        v.wait(aS)
        v.reduce_sum(MXC[:, 0:1], PRW[:], axis=AX.X, inc=vS)
        v.wait(vS)
        v.reciprocal(MXC[:, 1:2], MXC[:, 0:1], inc=vS)
        v.wait(vS)
        v.tensor_scalar_mul(PRF[:], PRW[:], MXC[:, 1:2], inc=vS)
        v.wait(vS)
        v.tensor_copy(PRW[:], PRF[:], inc=vS)
        v.wait(vS)
        v.memset(MXC[:, 2:3], 0.0, inc=vS)
        v.wait(vS)
        for r6 in range(6):
            v.reduce_max(MXC[:, 0:1], PRW[:], axis=AX.X, inc=vS)
            v.wait(vS)
            v.tensor_add(MXC[:, 2:3], MXC[:, 2:3], MXC[:, 0:1], inc=vS)
            v.wait(vS)
            if r6 < 5:
                v.tensor_single_scalar(EQT[:], PRW[:], MXC[:, 0:1], ALU.is_ge,
                                       inc=vS)
                v.wait(vS)
                v.tensor_scalar(EQT[:], EQT[:], -BIG, 0.0, ALU.mult, ALU.add,
                                inc=vS)
                v.wait(vS)
                v.tensor_add(PRW[:], PRW[:], EQT[:], inc=vS)
                v.wait(vS)
            else:
                v.tensor_copy(MXC[:, 3:4], MXC[:, 0:1], inc=vS)
                v.wait(vS)
        v.tensor_scalar_add(SDI[:, 0:1], MXC[:, 2:3], 1e-10, inc=vS)
        v.wait(vS)
        v.reciprocal(SDI[:, 1:2], SDI[:, 0:1], inc=vS)
        v.wait(vS)
        v.tensor_mul(SDI[:, 0:1], MXC[:, 2:3], SDI[:, 1:2], inc=vS)
        v.wait(vS)
        v.tensor_single_scalar(EQT[:], PRF[:], MXC[:, 3:4], ALU.is_ge, inc=vS)
        v.wait(vS)
        v.tensor_mul(EQT[:], EQT[:], PRF[:], inc=vS)
        v.wait(vS)
        v.tensor_scalar_mul(EQT[:], EQT[:], SDI[:, 1:2], inc=vS)
        v.wait(vS)
        v.reciprocal(MXC[:, 0:1], SDI[:, 0:1], inc=vS)
        v.wait(vS)
        v.tensor_scalar_mul(EQT[:], EQT[:], MXC[:, 0:1], inc=vS)
        v.wait(vS)
        v.tensor_copy(GFB[:, 0:16], EQT[:], inc=vS)
        v.wait(vS)
        v.tensor_copy(GFB[:, 16:17], SDI[:, 0:1], inc=vS)
        t.wait(vS)
        ctx.bank_acquire([2])
        tp = ctx.banks[2].bitcast(BF16)[0:17, 0:128]
        t.transpose(tp, GFB[:], EYE[:], inc=mS)
        v.wait(mS)
        v.tensor_copy(GT[:, tc * 128:(tc + 1) * 128], tp, inc=vS)
        ctx.bank_release([2], vS)
    for tt in range(NTT):
        ctx.bank_acquire([3])
        t.wait(vS)
        t.matmul(ctx.banks[3], ESEL[:, 16 * 128:17 * 128],
                 GT[:, tt * TT:(tt + 1) * TT], start=True, stop=True, inc=mS)
        v.wait(mS)
        v.tensor_copy(ACCB[:, tt * TT:(tt + 1) * TT], ctx.banks[3], inc=vS)
        ctx.bank_release([3], vS)
    ctx.barrier(("tensor", "vector", "scalar", "gpsimd"))

    ew = dr["ew"]
    ld_tgt = [0, 0]
    we_free = [0, 0]
    xe_free = [0, 0]
    for tt in range(NTT):
        for e in range(16):
            ctx.bank_acquire([0])
            t.wait(vS, vS.n)
            t.matmul(ctx.banks[0], ESEL[:, e * 128:(e + 1) * 128],
                     GT[:, tt * TT:(tt + 1) * TT], start=True, stop=True, inc=mS)
            v.wait(mS)
            v.tensor_copy(GEB[:, e * TT:(e + 1) * TT], ctx.banks[0], inc=vS)
            ctx.bank_release([0], vS)
        for e in range(16):
            sl = e % 2
            if xe_free[sl]:
                v.wait(mS, xe_free[sl])
            v.wait(vS, vS.n)
            for k in range(NCH):
                v.tensor_mul(XE[sl][:, k * TT:(k + 1) * TT],
                             XN[:, k * T + tt * TT: k * T + (tt + 1) * TT],
                             GEB[:, e * TT:(e + 1) * TT],
                             inc=(vS if k == NCH - 1 else None))
            t.wait(vS)
            for oc in range(NCH):
                wsl = oc % 2
                qe = ctx.wq if ew.tensor.dtype == BF16 else g
                lwS = ctx.sem("m_lw") if ew.tensor.dtype == BF16 else ldS
                if ld_tgt[wsl]:
                    qe.wait(mS, we_free[wsl])
                qe.dma_start(
                    out=ctx.WSLOT[wsl][:].bitcast(BF16)[:, 0:1024].rearrange(
                        "p (k m) -> p k m", k=8),
                    in_=ew[e * 1024:(e + 1) * 1024,
                           oc * 128:(oc + 1) * 128].rearrange(
                        "(k p) m -> p k m", p=128),
                    inc=lwS, inc_by=16)
                ld_tgt[wsl] = lwS.n
                t.wait(lwS, ld_tgt[wsl])
                if e == 0:
                    ctx.bank_acquire([oc])
                for k in range(NCH):
                    t.matmul(ctx.banks[oc],
                             ctx.WSLOT[wsl][:].bitcast(BF16)[:, k * 128:(k + 1) * 128],
                             XE[sl][:, k * TT:(k + 1) * TT],
                             start=(e == 0 and k == 0),
                             stop=(e == 15 and k == NCH - 1),
                             inc=(mS if k == NCH - 1 else None))
                we_free[wsl] = mS.n
            xe_free[sl] = mS.n
        for oc in range(NCH):
            v.wait(mS)
            v.wait(aS, aS.n)
            v.scalar_tensor_tensor(TMPM[:], ACCB[:, tt * TT:(tt + 1) * TT],
                                   1e-10, ctx.banks[oc], ALU.mult, ALU.add,
                                   inc=vS)
            s.wait(vS)
            s.activation(GG[:, oc * T + tt * TT: oc * T + (tt + 1) * TT],
                         TMPM[:], AF.Silu, inc=aS)
            ctx.bank_release([oc], aS)
    ctx.scratch_release(mark)


def build_ffn2_phase(dbg=False):
    specs = [("xn", [D, T], BF16, "ExternalInput"),
             ("rwt", [D, 16], F32, "ExternalInput"),
             ("rbb", [1, 16], BF16, "ExternalInput"),
             ("ew", [16 * D, D], F32, "ExternalInput"),
             ("esel", [17, 17 * 128], BF16, "ExternalInput"),
             ("eye", [128, 128], BF16, "ExternalInput"),
             ("gg", [D, T], BF16, "ExternalOutput")]
    nc, ctx, dr = _phase_nc(specs)
    C = setup_common(ctx)
    emit_load_fm(ctx, C["XN"], dr["xn"])
    for e in ENGS:
        ctx.eng[e].wait(ctx.sdma)
    ctx.barrier(ENGS)
    emit_ffn2(ctx, C, dr)
    ctx.barrier(ENGS)
    emit_store_fm(ctx, dr["gg"], C["GG"])
    if dbg:
        gto = nc.dram_tensor("gto", [17, T], BF16, kind="ExternalOutput").ap()
        ctx.y.dma_start(out=gto, in_=ctx._m_gt[:], inc=ctx.sdma, inc_by=16)
        for nm, ap in zip(("prfo", "mxco", "sdio", "eqto"), ctx._m_dbg):
            o = nc.dram_tensor(nm, list(ap.shape), F32, kind="ExternalOutput").ap()
            ctx.y.dma_start(out=o, in_=ap[:], inc=ctx.sdma, inc_by=16)
    ctx.y.wait(ctx.sdma)
    return ctx.finalize()

def emit_attn1(ctx, C, XN, PXN, OSB, dr):
    t, v, s, g = ctx.t, ctx.v, ctx.s, ctx.g
    mark = ctx.scratch_mark()
    QF = ctx.scratch("1qf", [128, S], BF16)
    VT = ctx.scratch("1vt", [128, NKC * 130], BF16)
    KT = ctx.scratch("1kt", [128, NKC * 128], BF16)
    KVT = ctx.scratch("1kv", [128, 65], BF16)
    PHT = ctx.scratch("1ph", [128, TT], F32)
    PHT2 = ctx.scratch("1ph2", [128, TT], F32)
    TO1 = ctx.scratch("1to1", [64, TT], F32)
    RC1 = ctx.scratch("1rc1", [1, TT], F32)
    RCB = ctx.scratch("1rcb", [1, TT], BF16)
    OH = ctx.scratch("1oh", [64, S], BF16)
    WQB = [ctx.scratch(f"1wq{i}", [128, NCH * 128], BF16) for i in range(3)]
    BQ1 = ctx.scratch("1bq", [128, 12], F32)
    vS, aS, mS, ldS, gS = (ctx.sem("1v"), ctx.sem("1a"), ctx.sem("1m"),
                           ctx.sem("1ld"), ctx.sem("1g"))
    g.dma_start(out=BQ1[:, 0:4], in_=dr["bq1"], inc=ldS, inc_by=16)
    g.dma_start(out=BQ1[:, 4:8], in_=dr["bk1"], inc=ldS, inc_by=16)
    g.dma_start(out=BQ1[:, 8:12], in_=dr["bv1"], inc=ldS, inc_by=16)

    def phi_fm(dst_slice, w):
        v.tensor_scalar_min(PHT[:, 0:w], dst_slice, 0.0)
        v.tensor_scalar_max(PHT2[:, 0:w], dst_slice, 0.0, inc=vS)
        s.wait(vS)
        s.activation(dst_slice, PHT[:, 0:w], AF.Exp, inc=aS)
        v.wait(aS)
        v.tensor_add(dst_slice, dst_slice, PHT2[:, 0:w], inc=vS)

    def qk_rhs(k, tt):
        if tt < NTT:
            return XN[:, k * T + tt * TT: k * T + (tt + 1) * TT]
        return PXN[:, k * T + (tt - NTT) * TT: k * T + (tt - NTT + 1) * TT]

    for hp in range(4):
        def evac_q(oc, tt, ps, mmS, evS, hp=hp):
            s.wait(mmS)
            s.wait(ldS, 48)
            s.activation(QF[:, tt * TT:(tt + 1) * TT], ps, AF.Identity,
                         bias=BQ1[:, hp:hp + 1], inc=evS)
        emit_proj(ctx, f"1q{hp}", dr["wq1"][:, hp * 128:(hp + 1) * 128], D, 128,
                  qk_rhs, evac_q, ntt=2 * NTT)
        ctx.barrier(("tensor", "vector", "scalar", "gpsimd"))
        for tt in range(2 * NTT):
            phi_fm(QF[:, tt * TT:(tt + 1) * TT], TT)
        for wi, wname in ((1, "wk1"), (2, "wv1")):
            sl_ = wi
            g.dma_start(out=WQB[sl_][:].rearrange("p (k m) -> p k m", k=NCH),
                        in_=dr[wname][:, hp * 128:(hp + 1) * 128].rearrange(
                            "(k p) m -> p k m", p=128),
                        inc=ldS, inc_by=16)
        g.memset(VT[:], 1.0, inc=gS)
        t.wait(ldS)
        t.wait(gS)
        VT3 = VT[:].rearrange("p (kc h j) -> p kc h j", kc=NKC, h=2)
        for kc in range(NKC):
            for wi, dstsel in ((1, "k"), (2, "v")):
                bi = wi - 1
                ctx.bank_acquire([bi])
                ps = ctx.banks[bi][:, 0:128]
                for k in range(NCH):
                    if kc < 16:
                        lhsT = XN[:, k * T + kc * 128: k * T + (kc + 1) * 128]
                    else:
                        lhsT = PXN[:, k * T + (kc - 16) * 128: k * T + (kc - 15) * 128]
                    t.matmul(ps, lhsT, WQB[wi][:, k * 128:(k + 1) * 128],
                             start=(k == 0), stop=(k == NCH - 1),
                             inc=(mS if k == NCH - 1 else None))
                s.wait(mS)
                if dstsel == "k":
                    s.activation(KT[:, kc * 128:(kc + 1) * 128], ps, AF.Identity,
                                 bias=BQ1[:, 4 + hp:5 + hp], inc=aS)
                else:
                    s.activation(PHT[:, 0:128], ps, AF.Identity,
                                 bias=BQ1[:, 8 + hp:9 + hp], inc=aS)
                    v.wait(aS)
                    v.tensor_copy(VT3[:, kc, :, 0:64],
                                  PHT[:, 0:128].rearrange("p (h j) -> p h j", h=2),
                                  inc=vS)
                ctx.bank_release([bi], aS if dstsel == "k" else vS)
        ctx.barrier(("tensor", "vector", "scalar", "gpsimd"))
        for kc in range(0, NKC, 4):
            phi_fm(KT[:, kc * 128:(kc + 4) * 128], 512)
        ctx.barrier(("tensor", "vector", "scalar", "gpsimd"))
        for i in range(2):
            hh = 2 * hp + i
            base = 64 * i
            ctx.bank_acquire([2])
            kvp = ctx.banks[2][0:64, 0:65]
            for kc in range(NKC):
                t.matmul(kvp, KT[:, kc * 128 + base: kc * 128 + base + 64],
                         VT[:, kc * 130 + i * 65: kc * 130 + (i + 1) * 65],
                         start=(kc == 0), stop=(kc == NKC - 1),
                         inc=(mS if kc == NKC - 1 else None))
            v.wait(mS)
            v.tensor_copy(KVT[base:base + 64, :], kvp, inc=vS)
            ctx.bank_release([2], vS)
            t.wait(vS)
            for tt in range(2 * NTT):
                ctx.bank_acquire([3 + tt % 2])
                op = ctx.banks[3 + tt % 2][0:65, 0:TT]
                t.matmul(op, KVT[base:base + 64, :],
                         QF[base:base + 64, tt * TT:(tt + 1) * TT],
                         start=True, stop=True, inc=mS)
                v.wait(mS)
                v.tensor_scalar_max(RC1[:], op[64:65, :], 1e-6, inc=vS)
                v.wait(vS)
                v.reciprocal(RC1[:], RC1[:], inc=vS)
                v.wait(vS)
                v.tensor_copy(RCB[:], RC1[:], inc=vS)
                t.wait(vS)
                ctx.bank_acquire([5 + tt % 2])
                rb = ctx.banks[5 + tt % 2][0:64, 0:TT]
                t.matmul(rb, C["ones1x128_bf"][0:1, 0:64], RCB[:],
                         start=True, stop=True, inc=mS)
                v.wait(mS)
                v.tensor_copy(TO1[:], op[0:64, :], inc=vS)
                v.tensor_mul(OH[:, tt * TT:(tt + 1) * TT], TO1[:], rb, inc=vS)
                ctx.bank_release([3 + tt % 2], vS)
                ctx.bank_release([5 + tt % 2], vS)
            OH3 = OH[:].rearrange("p (a r) -> p a r", r=16)
            for r in range(16):
                c0 = (64 * r) // 128
                rb_ = (64 * r) % 128
                v.tensor_copy(
                    OSB[rb_:rb_ + 64, c0 * T + hh * 256: c0 * T + hh * 256 + 256],
                    OH3[0:64, :, r], inc=vS)
        ctx.barrier(("tensor", "vector", "scalar", "gpsimd"))
    ctx.scratch_release(mark)


def emit_l1_combine(ctx, C, dr, OSB, x_dram, xo_dram):
    t, v, s, g = ctx.t, ctx.v, ctx.s, ctx.g
    mark = ctx.scratch_mark()
    SRW = ctx.scratch("1srw", [128, NCH * T], BF16)
    BO = ctx.scratch("1bo", [128, NCH], F32)
    LG = ctx.scratch("1lg", [128, NCH], F32)
    LB = ctx.scratch("1lb", [128, NCH], F32)
    SQ = ctx.scratch("1sq", [128, TT], BF16)
    TMP = ctx.scratch("1tmp", [128, TT], F32)
    TMP2 = ctx.scratch("1tmp2", [128, TT], F32)
    XT = [ctx.scratch(f"1xt{i}", [128, TT], F32) for i in range(2)]
    ldS = ctx.sem("1cld")
    vS, aS, mS = ctx.sem("1cv"), ctx.sem("1ca"), ctx.sem("1cm")
    xload = ctx.sem("1cxl")
    g.dma_start(out=BO[:], in_=dr["bproj"], inc=ldS, inc_by=16)
    g.dma_start(out=LG[:], in_=dr["gamma1"], inc=ldS, inc_by=16)
    g.dma_start(out=LB[:], in_=dr["beta1"], inc=ldS, inc_by=16)

    def evac(oc, tt, ps, mmS, evS):
        s.wait(mmS)
        s.wait(ldS, 16)
        s.activation(SRW[:, oc * T + tt * TT: oc * T + (tt + 1) * TT], ps,
                     AF.Identity, bias=BO[:, oc:oc + 1], inc=evS)

    emit_proj(ctx, "1wo", dr["wproj1"], D, D,
              lambda k, tt: OSB[:, k * T + tt * TT: k * T + (tt + 1) * TT], evac)
    ctx.barrier(("tensor", "vector", "scalar", "gpsimd"))
    mu = ctx.statrow(); ex2 = ctx.statrow(); msq = ctx.statrow()
    std = ctx.statrow(); rstd = ctx.statrow(); nmu = ctx.statrow()
    bsum, bsq = ctx.banks[0][0:1, :], ctx.banks[1][0:1, :]
    brb, bmb = ctx.banks[2], ctx.banks[3]
    s.wait(ldS, 48)
    x3 = x_dram.rearrange("(c p) t -> p c t", p=128)
    xo3 = xo_dram.rearrange("(c p) t -> p c t", p=128)
    xld_tgt = {}
    for tt in range(NTT):
        for c in range(NCH):
            xa = SRW[:, c * T + tt * TT: c * T + (tt + 1) * TT]
            if c >= 1 or tt > 0:
                s.wait(mS)
            s.activation(SQ[:], xa, AF.Square, inc=aS)
            t.wait(aS)
            if c == 0:
                ctx.bank_acquire([0, 1])
            t.matmul(bsum, C["ones128_bf"], xa, start=(c == 0), stop=(c == NCH - 1))
            t.matmul(bsq, C["ones128_bf"], SQ[:], start=(c == 0),
                     stop=(c == NCH - 1), inc=mS)
        v.wait(mS)
        v.tensor_scalar_mul(mu, bsum, 1.0 / D, inc=vS)
        v.tensor_scalar_mul(ex2, bsq, 1.0 / D, inc=vS)
        ctx.bank_release([0, 1], vS)
        v.tensor_mul(msq, mu, mu)
        v.tensor_sub(ex2, ex2, msq)
        v.tensor_scalar_add(ex2, ex2, LN_EPS, inc=vS)
        s.wait(vS)
        s.activation(std, ex2, AF.Sqrt, inc=aS)
        v.wait(aS)
        v.reciprocal(rstd, std, inc=vS)
        v.wait(vS)
        v.scalar_tensor_tensor(nmu, mu, -1.0, rstd, ALU.mult, ALU.mult, inc=vS)
        t.wait(vS)
        ctx.bank_acquire([2, 3])
        t.matmul(brb, C["ones1x128_f32"], rstd, start=True, stop=True)
        t.matmul(bmb, C["ones1x128_f32"], nmu, start=True, stop=True, inc=mS)
        v.wait(mS)
        for c in range(NCH):
            col = c * T + tt * TT
            sl = (tt * NCH + c) % 2
            g.wait(ctx.gdma, xld_tgt.get((sl, "st"), 0))
            g.dma_start(out=XT[sl][:], in_=x3[:, c, tt * TT:(tt + 1) * TT],
                        inc=xload, inc_by=16)
            xld_tgt[sl] = xload.n
            sr = SRW[:, col:col + TT]
            v.tensor_copy(TMP[:], sr)
            v.tensor_mul(TMP2[:], TMP[:], brb)
            v.tensor_add(TMP[:], TMP2[:], bmb, inc=vS)
            s.wait(vS)
            s.activation(TMP2[:], TMP[:], AF.Identity, bias=LB[:, c:c + 1],
                         scale=LG[:, c:c + 1], inc=aS)
            v.wait(aS)
            v.tensor_add(TMP[:], TMP2[:], C["XN"][:, col:col + TT])
            v.tensor_mul(TMP2[:], TMP[:], C["GG"][:, col:col + TT], inc=vS)
            v.wait(xload, xld_tgt[sl])
            v.tensor_add(XT[sl][:], XT[sl][:], TMP2[:], inc=vS)
            g.wait(vS)
            g.dma_start(out=xo3[:, c, tt * TT:(tt + 1) * TT], in_=XT[sl][:],
                        inc=ctx.gdma, inc_by=16)
            xld_tgt[(sl, "st")] = ctx.gdma.n
        ctx.bank_release([2, 3], vS)
        t.wait(vS)
    g.wait(ctx.gdma)
    ctx.scratch_release(mark)


def build_attn1_phase():
    specs = [("x", [D, T], F32, "ExternalInput"),
             ("xn", [D, T], BF16, "ExternalInput"),
             ("pxn", [D, T], BF16, "ExternalInput"),
             ("gg", [D, T], BF16, "ExternalInput"),
             ("wq1", [D, 512], F32, "ExternalInput"),
             ("wk1", [D, 512], F32, "ExternalInput"),
             ("wv1", [D, 512], F32, "ExternalInput"),
             ("bq1", [128, 4], F32, "ExternalInput"),
             ("bk1", [128, 4], F32, "ExternalInput"),
             ("bv1", [128, 4], F32, "ExternalInput"),
             ("wproj1", [D, D], F32, "ExternalInput"),
             ("bproj", [128, NCH], F32, "ExternalInput"),
             ("gamma1", [128, NCH], F32, "ExternalInput"),
             ("beta1", [128, NCH], F32, "ExternalInput"),
             ("xo", [D, T], F32, "ExternalOutput")]
    nc, ctx, dr = _phase_nc(specs)
    C = setup_common(ctx)
    OSB = ctx.scratch("OSB", [128, NCH * T], BF16)
    ctx._scr0 = ctx._scr
    mark_pxn = ctx.scratch_mark()
    PXN = ctx.scratch("PXN", [128, NCH * T], BF16)
    emit_load_fm(ctx, C["XN"], dr["xn"])
    emit_load_fm(ctx, PXN, dr["pxn"])
    emit_load_fm(ctx, C["GG"], dr["gg"])
    for e in ENGS:
        ctx.eng[e].wait(ctx.sdma)
    ctx.barrier(ENGS)
    emit_attn1(ctx, C, C["XN"], PXN, OSB, dr)
    ctx.barrier(ENGS)
    ctx.scratch_release(mark_pxn)
    emit_l1_combine(ctx, C, dr, OSB, dr["x"], dr["xo"])
    ctx.barrier(ENGS)
    ctx.y.wait(ctx.gdma)
    return ctx.finalize()

def emit_ffn1(ctx, C, dr, hsp):
    t, v, s, g = ctx.t, ctx.v, ctx.s, ctx.g
    XN, GG = C["XN"], C["GG"]
    mark = ctx.scratch_mark()
    S1C = ctx.scratch("n_s1", [128, 32], F32)
    S2C = ctx.scratch("n_s2", [128, 32], F32)
    H1C = ctx.scratch("n_h1s", [128, 32], F32)
    H2C = ctx.scratch("n_h2s", [128, 32], F32)
    W3C = ctx.scratch("n_w3s", [128, NCH], F32)
    TA = ctx.scratch("n_ta", [128, TT], BF16)
    TB = ctx.scratch("n_tb", [128, TT], BF16)
    HBF = [ctx.scratch(f"n_hb{i}", [128, T], BF16) for i in range(2)]
    HQ = ctx.scratch("n_hq", [128, 32 * TT], BF16)
    H1Q = ctx.scratch("n_h1q", [128, 32 * TT], BF16)
    vS, aS, mS, ldS = (ctx.sem("n_v"), ctx.sem("n_a"), ctx.sem("n_m"),
                       ctx.sem("n_ld"))
    for nm, buf in (("s1c", S1C), ("s2c", S2C), ("h1sc", H1C), ("h2sc", H2C),
                    ("w3sc", W3C)):
        g.dma_start(out=buf[:], in_=dr[nm], inc=ldS, inc_by=16)
    hb_st = [0, 0]
    wfree = [0, 0]
    ldt = [0, 0]
    for oc in range(32):
        sl = oc % 2
        qe = ctx.wq if dr["w1t"].tensor.dtype == BF16 else g
        lwS = ctx.sem("n_lw") if dr["w1t"].tensor.dtype == BF16 else ldS
        if wfree[sl]:
            qe.wait(mS, wfree[sl])
        for br, wname in ((0, "w1t"), (1, "w2t")):
            qe.dma_start(
                out=ctx.WSLOT[sl][:].bitcast(BF16)[:, br * 1024:(br + 1) * 1024]
                    .rearrange("p (k m) -> p k m", k=8),
                in_=dr[wname][:, oc * 128:(oc + 1) * 128].rearrange(
                    "(k p) m -> p k m", p=128),
                inc=lwS, inc_by=16)
        ldt[sl] = lwS.n
        t.wait(lwS, ldt[sl])
        if hb_st[sl]:
            s.wait(ctx.gdma, hb_st[sl])
        for tt in range(NTT):
            bA, bB = ctx.banks[sl * 2], ctx.banks[sl * 2 + 1]
            ctx.bank_acquire([sl * 2, sl * 2 + 1])
            for br in range(2):
                ps = (bA, bB)[br]
                for k in range(NCH):
                    t.matmul(ps, ctx.WSLOT[sl][:].bitcast(BF16)[
                                 :, br * 1024 + k * 128: br * 1024 + (k + 1) * 128],
                             XN[:, k * T + tt * TT: k * T + (tt + 1) * TT],
                             start=(k == 0), stop=(k == NCH - 1),
                             inc=(mS if k == NCH - 1 else None))
            wfree[sl] = mS.n
            s.wait(mS)
            s.activation(TA[:], bA, AF.Silu, scale=S1C[:, oc:oc + 1], inc=aS)
            s.activation(TB[:], bB, AF.Identity, scale=S2C[:, oc:oc + 1], inc=aS)
            v.wait(aS)
            v.tensor_mul(HBF[sl][:, tt * TT:(tt + 1) * TT], TA[:], TB[:], inc=vS)
            ctx.bank_release([sl * 2, sl * 2 + 1], vS)
        g.wait(vS)
        g.dma_start(out=hsp[oc * 128:(oc + 1) * 128, :], in_=HBF[sl][:],
                    inc=ctx.gdma, inc_by=16)
        hb_st[sl] = ctx.gdma.n
    ctx.barrier(("tensor", "vector", "scalar", "gpsimd"))
    hq_ld = ctx.sem("n_hq")
    for q in range(NTT):
        g.dma_start(out=HQ[:].rearrange("p (k t) -> p k t", k=32),
                    in_=hsp[:, q * TT:(q + 1) * TT].rearrange(
                        "(k p) t -> p k t", p=128),
                    inc=hq_ld, inc_by=16)
        for stage, (wname, scl, nk, noc, src, dst) in enumerate((
                ("h1t", H1C, 32, 32, HQ, H1Q),
                ("h2t", H2C, 32, 32, H1Q, HQ),
                ("w3t", W3C, 32, NCH, HQ, None))):
            if stage == 0:
                for e in ("tensor", "vector", "scalar"):
                    ctx.eng[e].wait(hq_ld)
            for oc in range(noc):
                sl = oc % 2
                qe = ctx.wq if dr[wname].tensor.dtype == BF16 else g
                lwS = (ctx.sem("n_lw") if dr[wname].tensor.dtype == BF16
                       else ldS)
                if wfree[sl]:
                    qe.wait(mS, wfree[sl])
                qe.dma_start(
                    out=ctx.WSLOT[sl][:].bitcast(BF16)[:, 0:nk * 128].rearrange(
                        "p (k m) -> p k m", k=nk),
                    in_=dr[wname][:, oc * 128:(oc + 1) * 128].rearrange(
                        "(k p) m -> p k m", p=128),
                    inc=lwS, inc_by=16)
                ldt[sl] = lwS.n
                t.wait(lwS, ldt[sl])
                ctx.bank_acquire([sl])
                ps = ctx.banks[sl]
                for k in range(nk):
                    t.matmul(ps, ctx.WSLOT[sl][:].bitcast(BF16)[
                                 :, k * 128:(k + 1) * 128],
                             src[:, k * TT:(k + 1) * TT],
                             start=(k == 0), stop=(k == nk - 1),
                             inc=(mS if k == nk - 1 else None))
                wfree[sl] = mS.n
                s.wait(mS)
                if stage < 2:
                    s.activation(dst[:, oc * TT:(oc + 1) * TT], ps, AF.Gelu,
                                 scale=scl[:, oc:oc + 1], inc=aS)
                else:
                    s.activation(GG[:, oc * T + q * TT: oc * T + (q + 1) * TT],
                                 ps, AF.Silu, scale=scl[:, oc:oc + 1], inc=aS)
                ctx.bank_release([sl], aS)
            ctx.barrier(("tensor", "vector", "scalar", "gpsimd"))
    ctx.scratch_release(mark)


def build_ffn1_phase():
    specs = [("xn", [D, T], BF16, "ExternalInput"),
             ("w1t", [D, 4096], F32, "ExternalInput"),
             ("w2t", [D, 4096], F32, "ExternalInput"),
             ("h1t", [4096, 4096], F32, "ExternalInput"),
             ("h2t", [4096, 4096], F32, "ExternalInput"),
             ("w3t", [4096, D], F32, "ExternalInput"),
             ("s1c", [128, 32], F32, "ExternalInput"),
             ("s2c", [128, 32], F32, "ExternalInput"),
             ("h1sc", [128, 32], F32, "ExternalInput"),
             ("h2sc", [128, 32], F32, "ExternalInput"),
             ("w3sc", [128, NCH], F32, "ExternalInput"),
             ("gg", [D, T], BF16, "ExternalOutput")]
    nc, ctx, dr = _phase_nc(specs)
    C = setup_common(ctx)
    hsp = nc.dram_tensor("hsp1", [4096, T], BF16, kind="Internal").ap()
    emit_load_fm(ctx, C["XN"], dr["xn"])
    for e in ENGS:
        ctx.eng[e].wait(ctx.sdma)
    ctx.barrier(ENGS)
    emit_ffn1(ctx, C, dr, hsp)
    ctx.barrier(ENGS)
    emit_store_fm(ctx, dr["gg"], C["GG"])
    ctx.y.wait(ctx.sdma)
    return ctx.finalize()

def emit_attn0(ctx, C, XN, PXN, OSB, dr):
    t, v, s, g = ctx.t, ctx.v, ctx.s, ctx.g
    mark = ctx.scratch_mark()
    CQ = ctx.scratch("0cq", [32, T], BF16)
    CKV = ctx.scratch("0ckv", [64, S], BF16)
    K2 = ctx.scratch("0k2", [64, S], BF16)
    QA = ctx.scratch("0qa", [128, T], BF16)
    QB = ctx.scratch("0qb", [64, T], BF16)
    K1 = ctx.scratch("0k1", [128, S], BF16)
    VT = ctx.scratch("0vt", [128, NKC * 128], BF16)
    PRB = ctx.scratch("0prb", [128, 2 * TT], BF16)
    SUMS = ctx.scratch("0sums", [1, T], F32)
    KM1 = ctx.scratch("0km1", [128, 1], F32)
    KM1B = ctx.scratch("0km1b", [128, 1], BF16)
    KM2 = ctx.scratch("0km2", [32, 1], F32)
    KM2B = ctx.scratch("0km2b", [32, 1], BF16)
    QAB = ctx.scratch("0qab", [128, TT], BF16)
    QRB = ctx.scratch("0qrb", [32, TT], BF16)
    BRW = ctx.scratch("0brw", [1, TT], BF16)
    TO1 = ctx.scratch("0to1", [128, TT], BF16)
    RCF = ctx.scratch("0rcf", [1, TT], F32)
    BC = ctx.scratch("0bc", [128, 24], F32)
    vS, aS, mS, ldS, gS = (ctx.sem("0v"), ctx.sem("0a"), ctx.sem("0m"),
                           ctx.sem("0ld"), ctx.sem("0g"))
    wld = ctx.sem("0wld")
    g.dma_start(out=BC[:, 0:3], in_=dr["bsm"], inc=ldS, inc_by=16)
    g.dma_start(out=BC[:, 3:11], in_=dr["buq"], inc=ldS, inc_by=16)
    g.dma_start(out=BC[:, 11:19], in_=dr["buk"], inc=ldS, inc_by=16)
    g.dma_start(out=BC[:, 19:21], in_=dr["bqr"], inc=ldS, inc_by=16)

    def qk_rhs(k, tt):
        if tt < NTT:
            return XN[:, k * T + tt * TT: k * T + (tt + 1) * TT]
        return PXN[:, k * T + (tt - NTT) * TT: k * T + (tt - NTT + 1) * TT]

    def evac_to(dst, bias_col, rows=32):
        def evac(oc, tt, ps, mmS, evS):
            s.wait(mmS)
            s.wait(ldS, 64)
            s.activation(dst[0:rows, tt * TT:(tt + 1) * TT], ps[0:rows, :],
                         AF.Identity, bias=bias_col[0:rows, :], inc=evS)
        return evac

    emit_proj(ctx, "0cq", dr["wdq"], D, 32, qk_rhs,
              evac_to(CQ, BC[:, 0:1]), ntt=NTT)
    ctx.barrier(("tensor", "vector", "scalar", "gpsimd"))
    emit_proj(ctx, "0ckv", dr["wdkv"], D, 32, qk_rhs,
              evac_to(CKV, BC[:, 1:2]), ntt=2 * NTT)
    ctx.barrier(("tensor", "vector", "scalar", "gpsimd"))
    emit_proj(ctx, "0kr", dr["wkr"], D, 32, qk_rhs,
              evac_to(K2, BC[:, 2:3]), ntt=2 * NTT)
    ctx.barrier(("tensor", "vector", "scalar", "gpsimd"))
    g.memset(K2[32:34, :], 0.0)
    g.memset(K2[32:33, :], 1.0)
    g.memset(CKV[32:34, :], 0.0)
    g.memset(CKV[32:33, :], 1.0)
    g.memset(QB[32:34, 0:T], 0.0, inc=gS)
    for e in ("vector", "scalar", "tensor"):
        ctx.eng[e].wait(gS)
    v.reduce_max(KM2[:], K2[0:32, :], axis=AX.X,
                 apply_absolute_value=True, inc=vS)
    v.wait(vS)
    v.tensor_copy(KM2B[:], KM2[:], inc=vS)
    scale0 = 1.0 / math.sqrt(160.0)
    for hh in range(NCH):
        g.wait(aS, aS.n)
        g.dma_start(out=ctx.WSLOT[0][:].bitcast(BF16)[0:33, 0:384],
                    in_=dr["wu3"][:, hh * 384:(hh + 1) * 384],
                    inc=wld, inc_by=16)
        g.dma_start(out=ctx.WSLOT[0][:].bitcast(BF16)[0:32, 384:416],
                    in_=dr["wqr"][:, hh * 32:(hh + 1) * 32], inc=wld, inc_by=16)
        t.wait(wld)
        WQ = ctx.WSLOT[0][:].bitcast(BF16)
        for tt in range(NTT):
            ctx.bank_acquire([tt % 2])
            ps = ctx.banks[tt % 2]
            t.matmul(ps, WQ[0:32, 0:128], CQ[:, tt * TT:(tt + 1) * TT],
                     start=True, stop=True, inc=mS)
            s.wait(mS)
            s.activation(QA[:, tt * TT:(tt + 1) * TT], ps, AF.Identity,
                         bias=BC[:, 3 + hh:4 + hh], inc=aS)
            ctx.bank_release([tt % 2], aS)
        for tt in range(NTT):
            ctx.bank_acquire([2 + tt % 2])
            ps = ctx.banks[2 + tt % 2][0:32, :]
            t.matmul(ps, WQ[0:32, 384:416], CQ[:, tt * TT:(tt + 1) * TT],
                     start=True, stop=True, inc=mS)
            s.wait(mS)
            s.activation(QB[0:32, tt * TT:(tt + 1) * TT], ps, AF.Identity,
                         bias=BC[(hh % 4) * 32:(hh % 4) * 32 + 32,
                                 19 + hh // 4:20 + hh // 4], inc=aS)
            ctx.bank_release([2 + tt % 2], aS)
        for tt in range(2 * NTT):
            ctx.bank_acquire([4 + tt % 2])
            ps = ctx.banks[4 + tt % 2]
            t.matmul(ps, WQ[0:32, 128:256], CKV[0:32, tt * TT:(tt + 1) * TT],
                     start=True, stop=True, inc=mS)
            s.wait(mS)
            s.activation(K1[:, tt * TT:(tt + 1) * TT], ps, AF.Identity,
                         bias=BC[:, 11 + hh:12 + hh], inc=aS)
            ctx.bank_release([4 + tt % 2], aS)
        for kc in range(NKC):
            ctx.bank_acquire([6 + kc % 2])
            ps = ctx.banks[6 + kc % 2][:, 0:128]
            t.matmul(ps, CKV[0:33, kc * 128:(kc + 1) * 128], WQ[0:33, 256:384],
                     start=True, stop=True, inc=mS)
            s.wait(mS)
            s.activation(VT[:, kc * 128:(kc + 1) * 128], ps, AF.Identity, inc=aS)
            ctx.bank_release([6 + kc % 2], aS)
        v.wait(aS)
        v.reduce_max(KM1[:], K1[:], axis=AX.X, apply_absolute_value=True, inc=vS)
        v.wait(vS)
        v.tensor_copy(KM1B[:], KM1[:], inc=vS)
        v.wait(ctx.gdma)
        for tt in range(NTT):
            s.wait(vS, vS.n)
            s.wait(mS, mS.n)
            s.activation(QAB[:], QA[:, tt * TT:(tt + 1) * TT], AF.Abs, inc=aS)
            s.activation(QRB[:], QB[0:32, tt * TT:(tt + 1) * TT], AF.Abs, inc=aS)
            t.wait(aS)
            ctx.bank_acquire([0])
            bps = ctx.banks[0][0:1, :]
            t.matmul(bps, KM1B[:], QAB[:], start=True, stop=False)
            t.matmul(bps, KM2B[:], QRB[:], start=False, stop=True, inc=mS)
            v.wait(mS)
            v.wait(ctx.gdma)
            v.tensor_scalar_mul(BRW[:], bps, -1.0, inc=vS)
            ctx.bank_release([0], vS)
            g.wait(vS)
            g.dma_start(out=QB[32:33, tt * TT:(tt + 1) * TT], in_=BRW[:],
                        inc=ctx.gdma, inc_by=16)
        g.memset(SUMS[:], 0.0, inc=gS)
        t.wait(ctx.gdma)
        t.wait(gS)
        v.wait(gS)
        og = ctx.PG[1]
        for kc in range(NKC):
            for tp in range(2):
                ctx.bank_acquire([0, 1])
                sg2 = ctx.PG[0][:, 0:2 * TT]
                for ti in range(2):
                    tt = tp * 2 + ti
                    sl_ = sg2[:, ti * TT:(ti + 1) * TT]
                    t.matmul(sl_, K1[:, kc * 128:(kc + 1) * 128],
                             QA[:, tt * TT:(tt + 1) * TT], start=True, stop=False)
                    t.matmul(sl_, K2[0:34, kc * 128:(kc + 1) * 128],
                             QB[0:34, tt * TT:(tt + 1) * TT], start=False,
                             stop=True, inc=(mS if ti == 1 else None))
                s.wait(mS)
                s.activation(PRB[:], sg2, AF.Exp, scale=scale0, inc=aS)
                ctx.bank_release([0, 1], aS)
                t.wait(aS)
                for ti in range(2):
                    tt = tp * 2 + ti
                    if kc == 0:
                        ctx.bank_acquire([4 + tt])
                    t.matmul(og[:, tt * TT:(tt + 1) * TT],
                             VT[:, kc * 128:(kc + 1) * 128],
                             PRB[:, ti * TT:(ti + 1) * TT],
                             start=(kc == 0), stop=(kc == NKC - 1),
                             inc=(mS if kc == NKC - 1 and ti == 1 else None))
                    ctx.bank_acquire([2])
                    t.matmul(ctx.banks[2][0:1, :], C["ones128_bf"],
                             PRB[:, ti * TT:(ti + 1) * TT],
                             start=True, stop=True, inc=mS)
                    v.wait(mS)
                    v.tensor_add(SUMS[0:1, tt * TT:(tt + 1) * TT],
                                 SUMS[0:1, tt * TT:(tt + 1) * TT],
                                 ctx.banks[2][0:1, :], inc=vS)
                    ctx.bank_release([2], vS)
        v.wait(mS)
        for tt in range(NTT):
            v.wait(vS, vS.n)
            v.reciprocal(RCF[:], SUMS[0:1, tt * TT:(tt + 1) * TT], inc=vS)
            v.wait(vS)
            v.tensor_copy(BRW[:], RCF[:], inc=vS)
            t.wait(vS)
            ctx.bank_acquire([tt % 2])
            rb = ctx.banks[tt % 2]
            t.matmul(rb, C["ones1x128_bf"], BRW[:], start=True, stop=True, inc=mS)
            v.wait(mS)
            v.tensor_copy(TO1[:], og[:, tt * TT:(tt + 1) * TT], inc=vS)
            v.tensor_mul(OSB[:, hh * T + tt * TT: hh * T + (tt + 1) * TT],
                         TO1[:], rb, inc=vS)
            ctx.bank_release([tt % 2], vS)
            ctx.bank_release([4 + tt], vS)
    ctx.scratch_release(mark)


def build_attn0_phase():
    specs = [("x", [D, T], F32, "ExternalInput"),
             ("xn", [D, T], BF16, "ExternalInput"),
             ("pxn", [D, T], BF16, "ExternalInput"),
             ("gg", [D, T], BF16, "ExternalInput"),
             ("wdq", [D, 32], F32, "ExternalInput"),
             ("wdkv", [D, 32], F32, "ExternalInput"),
             ("wkr", [D, 32], F32, "ExternalInput"),
             ("bsm", [128, 3], F32, "ExternalInput"),
             ("wu3", [33, NCH * 384], F32, "ExternalInput"),
             ("wqr", [32, 256], F32, "ExternalInput"),
             ("buq", [128, NCH], F32, "ExternalInput"),
             ("buk", [128, NCH], F32, "ExternalInput"),
             ("bqr", [128, 2], F32, "ExternalInput"),
             ("wo", [D, D], F32, "ExternalInput"),
             ("bo", [128, NCH], F32, "ExternalInput"),
             ("xo", [D, T], F32, "ExternalOutput")]
    nc, ctx, dr = _phase_nc(specs)
    C = setup_common(ctx)
    OSB = ctx.scratch("OSB", [128, NCH * T], BF16)
    ctx._scr0 = ctx._scr
    mark_pxn = ctx.scratch_mark()
    PXN = ctx.scratch("PXN", [128, NCH * T], BF16)
    emit_load_fm(ctx, C["XN"], dr["xn"])
    emit_load_fm(ctx, PXN, dr["pxn"])
    emit_load_fm(ctx, C["GG"], dr["gg"])
    for e in ENGS:
        ctx.eng[e].wait(ctx.sdma)
    ctx.barrier(ENGS)
    emit_attn0(ctx, C, C["XN"], PXN, OSB, dr)
    ctx.barrier(ENGS)
    ctx.scratch_release(mark_pxn)
    XRES = ctx.scratch("XRES", [128, NCH * T], F32)
    emit_load_fm(ctx, XRES, dr["x"])
    for e in ("vector", "scalar"):
        ctx.eng[e].wait(ctx.sdma)
    emit_combine(ctx, C, dr["wo"], dr["bo"], OSB, C["GG"], XRES)
    ctx.barrier(ENGS)
    emit_store_fm(ctx, dr["xo"], XRES)
    ctx.y.wait(ctx.sdma)
    return ctx.finalize()

W0 = 2306
KC0 = 18
KW = [512, 512, 512, 512, 258]

def emit_ffn0_attn(ctx, C, dr, YF, YB):
    t, v, s, g = ctx.t, ctx.v, ctx.s, ctx.g
    mark = ctx.scratch_mark()
    OSBI = ctx.scratch("i_osb", [128, NCH * T], BF16)
    QFI = ctx.scratch("i_qf", [128, T], BF16)
    KFI = ctx.scratch("i_kf", [128, W0], BF16)
    VTI = ctx.scratch("i_vt", [128, KC0 * 65], BF16)
    SQI = ctx.scratch("i_sq", [64, TT], BF16)
    RWT = ctx.scratch("i_rwt", [1, TT], BF16)
    NRM = ctx.scratch("i_nrm", [1, TT], F32)
    PRBI = ctx.scratch("i_prb", [128, 1024], BF16)
    TQ = ctx.scratch("i_tq", [64, TT], F32)
    TQ2 = ctx.scratch("i_tq2", [64, TT], F32)
    KMXI = ctx.scratch("i_kmx", [64, 2], F32)
    KMBI = ctx.scratch("i_kmb", [64, 2], BF16)
    QABI = ctx.scratch("i_qab", [64, TT], BF16)
    BRWI = ctx.scratch("i_brw", [1, TT], BF16)
    TO1I = ctx.scratch("i_to1", [64, 256], F32)
    RCFI = ctx.scratch("i_rcf", [1, 256], F32)
    SCC = ctx.scratch("i_scc", [64, 32], F32)
    ONESI = ctx.scratch("i_ones", [1, TT], BF16)
    RC4 = [ctx.scratch(f"i_t{j}", [64, W0], BF16) for j in range(4)]
    vS, aS, mS, ldS, gS = (ctx.sem("i_v"), ctx.sem("i_a"), ctx.sem("i_m"),
                           ctx.sem("i_ld"), ctx.sem("i_g"))
    wld = ctx.sem("i_wld")
    EYE64 = ctx.scratch("i_eye", [64, 64], BF16)
    KMXB = ctx.scratch("i_kmxb", [64, 16], BF16)
    B0ROW = ctx.scratch("i_b0row", [1, 16], F32)
    g.dma_start(out=SCC[:, 0:16], in_=dr["qsc"], inc=ldS, inc_by=16)
    g.dma_start(out=SCC[:, 16:32], in_=dr["ksc"], inc=ldS, inc_by=16)
    for j, nm in enumerate(("rc", "rssq", "rci", "rssi")):
        g.dma_start(out=RC4[j][:], in_=dr[nm], inc=ldS, inc_by=16)
    g.memset(EYE64[:], 1.0, inc=gS)
    g.wait(gS)
    g.affine_select(out=EYE64[:], in_=EYE64[:], pattern=[[-1, 64]],
                    compare_op=ALU.is_equal, fill=0.0, base=0,
                    channel_multiplier=1, inc=gS)
    v.wait(gS)
    v.wait(ldS)
    v.tensor_mul(KMXB[:, 0:16], SCC[:, 0:16], SCC[:, 16:32], inc=vS)
    t.wait(vS)
    ctx.bank_acquire([6])
    pT = ctx.banks[6][0:16, 0:64]
    t.matmul(pT, KMXB[:, 0:16], EYE64[:], start=True, stop=True, inc=mS)
    v.wait(mS)
    v.reduce_max(KMXI[0:16, 0:1], pT, axis=AX.X, apply_absolute_value=True,
                 inc=vS)
    ctx.bank_release([6], vS)
    v.wait(vS)
    v.tensor_copy(KMBI[0:16, 0:1], KMXI[0:16, 0:1], inc=vS)
    t.wait(vS)
    ctx.bank_acquire([6])
    pR = ctx.banks[6][0:1, 0:16]
    t.matmul(pR, KMBI[0:16, 0:1], EYE64[0:16, 0:16], start=True, stop=True,
             inc=mS)
    v.wait(mS)
    v.tensor_scalar_mul(B0ROW[:], pR, -1.0, inc=vS)
    ctx.bank_release([6], vS)
    v.wait(vS)

    wqtgt = {}
    consA = {}
    consV = {}

    def wq_load(hh):
        if hh >= 16:
            return
        sl = hh % 2
        qe = ctx.wq if dr["wq3i"].tensor.dtype == BF16 else g
        en = "sync" if qe is ctx.wq else "gpsimd"
        li = ctx.last_inc[en]
        if hh - 2 in consA:
            qe.wait(aS, consA[hh - 2])
            qe.wait(vS, consV[hh - 2])
        qe.dma_start(out=ctx.WSLOT[sl][:].bitcast(BF16)[:, 0:8 * 192].rearrange(
                        "p (k m) -> p k m", k=8),
                    in_=dr["wq3i"][:, hh * 192:(hh + 1) * 192].rearrange(
                        "(k p) m -> p k m", p=128),
                    inc=wld, inc_by=16)
        wqtgt[hh] = wld.n
        ctx.last_inc[en] = li

    def z_rhs(k, c0, w):
        off = k * W0 + c0 - (1 if k >= 4 else 0)
        return YB[:, off:off + w]

    def lnorm_rot(dst, base_col, w, is_q, hh):
        s.wait(vS, vS.n)
        s.activation(SQI[0:64, 0:w], dst, AF.Square, inc=aS)
        t.wait(aS)
        ctx.bank_acquire([6])
        n2 = ctx.banks[6][0:1, 0:w]
        t.matmul(n2, C["ones128_bf"][0:64, :], SQI[0:64, 0:w],
                 start=True, stop=True, inc=mS)
        s.wait(mS)
        s.activation(NRM[0:1, 0:w], n2, AF.Sqrt, inc=aS)
        ctx.bank_release([6], aS)
        v.wait(aS)
        v.tensor_scalar_max(NRM[0:1, 0:w], NRM[0:1, 0:w], 1e-12, inc=vS)
        v.wait(vS)
        v.reciprocal(NRM[0:1, 0:w], NRM[0:1, 0:w], inc=vS)
        v.wait(vS)
        v.tensor_copy(RWT[0:1, 0:w], NRM[0:1, 0:w], inc=vS)
        t.wait(vS)
        ctx.bank_acquire([6])
        rb = ctx.banks[6][0:64, 0:w]
        t.matmul(rb, C["ones1x128_bf"][0:1, 0:64], RWT[0:1, 0:w],
                 start=True, stop=True, inc=mS)
        v.wait(mS)
        v.tensor_mul(TQ[0:64, 0:w], dst, rb, inc=vS)
        ctx.bank_release([6], vS)
        s.wait(vS)
        scol = SCC[:, hh:hh + 1] if is_q else SCC[:, 16 + hh:17 + hh]
        s.activation(TQ2[0:64, 0:w], TQ[0:64, 0:w], AF.Identity,
                     scale=scol[0:64, :], inc=aS)
        v.wait(aS)
        ct, st_ = (RC4[0], RC4[1]) if is_q else (RC4[2], RC4[3])
        v.tensor_mul(TQ[0:64, 0:w], TQ2[0:64, 0:w],
                     ct[0:64, base_col:base_col + w], inc=vS)
        v.tensor_mul(dst[0:32], TQ2[32:64, 0:w],
                     st_[32:64, base_col:base_col + w])
        v.tensor_mul(dst[32:64], TQ2[0:32, 0:w],
                     st_[0:32, base_col:base_col + w], inc=vS)
        v.wait(vS)
        v.tensor_add(dst, dst, TQ[0:64, 0:w], inc=vS)

    wq_load(0)
    for hh in range(16):
        g.memset(KFI[64:67, :], 0.0)
        g.memset(KFI[64:65, :], 1.0)
        g.memset(QFI[64:67, :], 0.0)
        g.memset(ONESI[:], 1.0, inc=gS)
        g.wait(gS)
        g.dma_start(out=KFI[65:66, :], in_=dr["hinv"], inc=ctx.gdma, inc_by=16)
        for tt in range(NTT):
            g.dma_start(out=QFI[65:66, tt * TT:(tt + 1) * TT],
                        in_=ONESI[0:1, 0:TT], inc=ctx.gdma, inc_by=16)
        t.wait(wld, wqtgt[hh])
        t.wait(gS, gS.n)
        t.wait(ctx.gdma)
        WQ = ctx.WSLOT[hh % 2][:].bitcast(BF16)
        for tt in range(NTT):
            ctx.bank_acquire([tt % 2])
            ps = ctx.banks[tt % 2][0:64, :]
            for k in range(NCH):
                t.matmul(ps, WQ[:, k * 192:k * 192 + 64],
                         z_rhs(k, 258 + tt * TT, TT),
                         start=(k == 0), stop=(k == NCH - 1),
                         inc=(mS if k == NCH - 1 else None))
            s.wait(mS)
            s.activation(QFI[0:64, tt * TT:(tt + 1) * TT], ps, AF.Identity,
                         inc=aS)
            ctx.bank_release([tt % 2], aS)
        kcol = 0
        for w in KW:
            ctx.bank_acquire([2])
            ps = ctx.banks[2][0:64, 0:w]
            for k in range(NCH):
                t.matmul(ps, WQ[:, k * 192 + 64:k * 192 + 128], z_rhs(k, kcol, w),
                         start=(k == 0), stop=(k == NCH - 1),
                         inc=(mS if k == NCH - 1 else None))
            s.wait(mS)
            s.activation(KFI[0:64, kcol:kcol + w], ps, AF.Identity, inc=aS)
            ctx.bank_release([2], aS)
            kcol += w
        g.wait(aS, aS.n)
        g.memset(VTI[:], 1.0, inc=gS)
        t.wait(gS)
        for kc in range(KC0):
            ctx.bank_acquire([3 + kc % 2])
            ps = ctx.banks[3 + kc % 2][:, 0:64]
            for k in range(NCH):
                zz = z_rhs(k, 2 + kc * 128, 128)
                t.matmul(ps, zz, WQ[:, k * 192 + 128:k * 192 + 192],
                         start=(k == 0), stop=(k == NCH - 1),
                         inc=(mS if k == NCH - 1 else None))
            v.wait(mS)
            v.tensor_copy(VTI[:, kc * 65:kc * 65 + 64], ps, inc=vS)
            ctx.bank_release([3 + kc % 2], vS)
        consA[hh] = aS.n
        consV[hh] = vS.n
        wq_load(hh + 1)
        for tt in range(NTT):
            lnorm_rot(QFI[0:64, tt * TT:(tt + 1) * TT], 258 + tt * TT, TT,
                      True, hh)
        kcol = 0
        for w in KW:
            lnorm_rot(KFI[0:64, kcol:kcol + w], kcol, w, False, hh)
            kcol += w
        v.wait(ctx.gdma)
        v.tensor_single_scalar(BRWI[0:1, 0:TT], ONESI[0:1, 0:TT],
                               B0ROW[0:1, hh:hh + 1], ALU.mult, inc=vS)
        g.wait(vS)
        for tt in range(NTT):
            g.dma_start(out=QFI[64:65, tt * TT:(tt + 1) * TT],
                        in_=BRWI[0:1, 0:TT], inc=ctx.gdma, inc_by=16)
        t.wait(ctx.gdma)
        og = ctx.PG[1]
        for qb in range(NCH):
            kbase = 256 * qb + 2
            ctx.bank_acquire([0, 1])
            sg2 = ctx.PG[0][:, 0:1024]
            for kci in range(4):
                t.matmul(sg2[:, kci * 256:(kci + 1) * 256],
                         KFI[0:67, kbase + kci * 128: kbase + (kci + 1) * 128],
                         QFI[0:67, qb * 256:(qb + 1) * 256],
                         start=True, stop=True,
                         inc=(mS if kci == 3 else None))
            s.wait(mS)
            s.activation(PRBI[:, 0:1024], sg2[:, 0:1024], AF.Exp, scale=8.0,
                         inc=aS)
            ctx.bank_release([0, 1], aS)
            g.wait(aS)
            for kci in range(4):
                kofs = qb * 256 + kci * 128 - 256
                g.affine_select(out=PRBI[:, kci * 256:(kci + 1) * 256],
                                in_=PRBI[:, kci * 256:(kci + 1) * 256],
                                pattern=[[1, 256]], compare_op=ALU.is_ge,
                                fill=0.0, base=qb * 256 - kofs,
                                channel_multiplier=-1)
                g.affine_select(out=PRBI[:, kci * 256:(kci + 1) * 256],
                                in_=PRBI[:, kci * 256:(kci + 1) * 256],
                                pattern=[[-1, 256]], compare_op=ALU.is_ge,
                                fill=0.0, base=256 - qb * 256 + kofs,
                                channel_multiplier=1,
                                inc=(gS if kci == 3 else None))
            t.wait(gS)
            if qb == 0:
                ctx.bank_acquire([4])
            elif qb == 1:
                ctx.bank_acquire([5])
            ob = ctx.banks[4 + qb % 2][0:65, 0:256]
            for kci in range(4):
                kc = qb * 2 + kci
                t.matmul(ob, VTI[:, kc * 65:(kc + 1) * 65],
                         PRBI[:, kci * 256:(kci + 1) * 256],
                         start=(kci == 0),
                         stop=(kci == 3), inc=(mS if kci == 3 else None))
            v.wait(mS)
            ov = ob
            v.reciprocal(RCFI[:], ov[64:65, :], inc=vS)
            v.wait(vS)
            v.tensor_copy(RWT[0:1, 0:256], RCFI[:], inc=vS)
            t.wait(vS)
            ctx.bank_acquire([6])
            rb = ctx.banks[6][0:64, 0:256]
            t.matmul(rb, C["ones1x128_bf"][0:1, 0:64], RWT[0:1, 0:256],
                     start=True, stop=True, inc=mS)
            v.wait(mS)
            v.tensor_copy(TO1I[:], ov[0:64, :], inc=vS)
            c = hh // 2
            rbase = (hh % 2) * 64
            v.tensor_mul(OSBI[rbase:rbase + 64,
                              c * T + qb * 256: c * T + (qb + 1) * 256],
                         TO1I[:], rb, inc=vS)
            ctx.bank_release([6], vS)
            ctx.bank_release([4 + qb % 2], vS)
        ctx.barrier(ENGS)
    def evac_wo(oc, tt, ps, mmS, evS):
        v.wait(mmS)
        yslice = YF[:, oc * W0 + 258 + tt * TT: oc * W0 + 258 + (tt + 1) * TT]
        ybs = YB[:, oc * W0 + 258 + tt * TT: oc * W0 + 258 + (tt + 1) * TT]
        v.tensor_add(yslice, yslice, ps, inc=evS)
        v.tensor_copy(ybs, yslice, inc=evS)

    emit_proj(ctx, "iwo", dr["woi"], D, D,
              lambda k, tt: OSBI[:, k * T + tt * TT: k * T + (tt + 1) * TT],
              evac_wo)
    ctx.scratch_release(mark)

def emit_ffn0_ffn(ctx, C, dr, YF2, last):
    t, v, s, g = ctx.t, ctx.v, ctx.s, ctx.g
    W1 = 2049
    mark = ctx.scratch_mark()
    ZN = ctx.scratch("f_zn", [128, NCH * T], BF16)
    UT = ctx.scratch("f_ut", [128, 22 * TT], BF16)
    TAF = ctx.scratch("f_ta", [128, TT], BF16)
    TBF = ctx.scratch("f_tb", [128, TT], BF16)
    XB = ctx.scratch("f_xb", [128, 2 * TT], BF16)
    SQ = ctx.scratch("f_sq", [128, 2 * TT], BF16)
    TMP = ctx.scratch("f_tmp", [128, TT], F32)
    TMP2 = ctx.scratch("f_tmp2", [128, TT], F32)
    B1A = ctx.scratch("f_b1a", [128, 22], F32)
    B1G = ctx.scratch("f_b1g", [128, 22], F32)
    LNG = ctx.scratch("f_lng", [128, NCH], F32)
    LNB = ctx.scratch("f_lnb", [128, NCH], F32)
    B2C = ctx.scratch("f_b2", [128, NCH], F32)
    vS, aS, mS, ldS = (ctx.sem("f_v"), ctx.sem("f_a"), ctx.sem("f_m"),
                       ctx.sem("f_ld"))
    for nm, buf in (("b1a", B1A), ("b1g", B1G), ("lngi", LNG), ("lnbi", LNB),
                    ("b2i", B2C)):
        g.dma_start(out=buf[:], in_=dr[nm], inc=ldS, inc_by=16)

    def z2(k, tt, w=TT):
        off = k * W1 + 1 + tt * TT - (1 if k >= 4 else 0)
        return YF2[:, off:off + w]

    mu = ctx.statrow(); ex2 = ctx.statrow(); msq = ctx.statrow()
    std = ctx.statrow(); rstd = ctx.statrow(); nmu = ctx.statrow()
    bsum, bsq = ctx.banks[0][0:1, :], ctx.banks[1][0:1, :]
    brb, bmb = ctx.banks[2], ctx.banks[3]
    s.wait(ldS, 80)
    for tt in range(NTT):
        for c in range(NCH):
            sl = c % 2
            if c >= 2 or tt > 0:
                v.wait(mS)
                s.wait(mS)
            v.tensor_copy(XB[:, sl * TT:(sl + 1) * TT], z2(c, tt), inc=vS)
            s.activation(SQ[:, sl * TT:(sl + 1) * TT], z2(c, tt), AF.Square,
                         inc=aS)
            t.wait(vS)
            t.wait(aS)
            if c == 0:
                ctx.bank_acquire([0, 1])
            t.matmul(bsum, C["ones128_bf"], XB[:, sl * TT:(sl + 1) * TT],
                     start=(c == 0), stop=(c == NCH - 1))
            t.matmul(bsq, C["ones128_bf"], SQ[:, sl * TT:(sl + 1) * TT],
                     start=(c == 0), stop=(c == NCH - 1), inc=mS)
        v.wait(mS)
        v.tensor_scalar_mul(mu, bsum, 1.0 / D, inc=vS)
        v.tensor_scalar_mul(ex2, bsq, 1.0 / D, inc=vS)
        ctx.bank_release([0, 1], vS)
        v.tensor_mul(msq, mu, mu)
        v.tensor_sub(ex2, ex2, msq)
        v.tensor_scalar_add(ex2, ex2, LN_EPS, inc=vS)
        s.wait(vS)
        s.activation(std, ex2, AF.Sqrt, inc=aS)
        v.wait(aS)
        v.reciprocal(rstd, std, inc=vS)
        v.wait(vS)
        v.scalar_tensor_tensor(nmu, mu, -1.0, rstd, ALU.mult, ALU.mult, inc=vS)
        t.wait(vS)
        ctx.bank_acquire([2, 3])
        t.matmul(brb, C["ones1x128_f32"], rstd, start=True, stop=True)
        t.matmul(bmb, C["ones1x128_f32"], nmu, start=True, stop=True, inc=mS)
        v.wait(mS)
        for c in range(NCH):
            v.tensor_mul(TMP[:], z2(c, tt), brb)
            v.tensor_add(TMP2[:], TMP[:], bmb, inc=vS)
            s.wait(vS)
            s.activation(ZN[:, c * T + tt * TT: c * T + (tt + 1) * TT], TMP2[:],
                         AF.Identity, scale=LNG[:, c:c + 1], bias=LNB[:, c:c + 1],
                         inc=aS)
            v.wait(aS)
        ctx.bank_release([2, 3], vS if False else aS)
        t.wait(aS)
    ctx.barrier(("tensor", "vector", "scalar", "gpsimd"))
    wfree = [0, 0]
    ldt = [0, 0]
    for tt in range(NTT):
        for uoc in range(22):
            sl = uoc % 2
            qe = ctx.wq if dr["w1a"].tensor.dtype == BF16 else g
            lwS = ctx.sem("f_lw") if dr["w1a"].tensor.dtype == BF16 else ldS
            if wfree[sl]:
                qe.wait(mS, wfree[sl])
            for br, wname in ((0, "w1a"), (1, "w1g")):
                qe.dma_start(
                    out=ctx.WSLOT[sl][:].bitcast(BF16)[:, br * 1024:(br + 1) * 1024]
                        .rearrange("p (k m) -> p k m", k=8),
                    in_=dr[wname][:, uoc * 128:(uoc + 1) * 128].rearrange(
                        "(k p) m -> p k m", p=128),
                    inc=lwS, inc_by=16)
            ldt[sl] = lwS.n
            t.wait(lwS, ldt[sl])
            ctx.bank_acquire([sl * 2, sl * 2 + 1])
            bA, bB = ctx.banks[sl * 2], ctx.banks[sl * 2 + 1]
            for br in range(2):
                ps = (bA, bB)[br]
                for k in range(NCH):
                    t.matmul(ps, ctx.WSLOT[sl][:].bitcast(BF16)[
                                 :, br * 1024 + k * 128: br * 1024 + (k + 1) * 128],
                             ZN[:, k * T + tt * TT: k * T + (tt + 1) * TT],
                             start=(k == 0), stop=(k == NCH - 1),
                             inc=(mS if k == NCH - 1 else None))
            wfree[sl] = mS.n
            s.wait(mS)
            s.activation(TAF[:], bA, AF.Identity, bias=B1A[:, uoc:uoc + 1],
                         inc=aS)
            s.activation(TBF[:], bB, AF.Gelu, bias=B1G[:, uoc:uoc + 1], inc=aS)
            v.wait(aS)
            v.tensor_mul(UT[:, uoc * TT:(uoc + 1) * TT], TAF[:], TBF[:], inc=vS)
            ctx.bank_release([sl * 2, sl * 2 + 1], vS)
        t.wait(vS)
        for oc in range(NCH):
            sl = oc % 2
            qe = ctx.wq if dr["w2p"].tensor.dtype == BF16 else g
            lwS = ctx.sem("f_lw") if dr["w2p"].tensor.dtype == BF16 else ldS
            if wfree[sl]:
                qe.wait(mS, wfree[sl])
            qe.dma_start(
                out=ctx.WSLOT[sl][:].bitcast(BF16)[:, 0:2816].rearrange(
                    "p (k m) -> p k m", k=22),
                in_=dr["w2p"][:, oc * 128:(oc + 1) * 128].rearrange(
                    "(k p) m -> p k m", p=128),
                inc=lwS, inc_by=16)
            ldt[sl] = lwS.n
            t.wait(lwS, ldt[sl])
            ctx.bank_acquire([4 + oc % 2])
            ps = ctx.banks[4 + oc % 2]
            for j in range(22):
                t.matmul(ps, ctx.WSLOT[sl][:].bitcast(BF16)[
                             :, j * 128:(j + 1) * 128],
                         UT[:, j * TT:(j + 1) * TT],
                         start=(j == 0), stop=(j == 21),
                         inc=(mS if j == 21 else None))
            wfree[sl] = mS.n
            s.wait(mS)
            s.activation(TMP[:], ps, AF.Identity, bias=B2C[:, oc:oc + 1], inc=aS)
            v.wait(aS)
            ys = YF2[:, oc * W1 + 1 + tt * TT: oc * W1 + 1 + (tt + 1) * TT]
            v.tensor_add(ys, ys, TMP[:], inc=vS)
            if last:
                s.wait(vS)
                s.activation(C["GG"][:, oc * T + tt * TT: oc * T + (tt + 1) * TT],
                             ys, AF.Silu, inc=aS)
            ctx.bank_release([4 + oc % 2], aS if last else vS)
    ctx.scratch_release(mark)


def build_ffn0a_phase(first=False):
    ydt = BF16 if first else F32
    specs = [("yown", [D, T], ydt, "ExternalInput"),
             ("yhalo", [D, 258], ydt, "ExternalInput"),
             ("wq3i", [D, 16 * 192], F32, "ExternalInput"),
             ("woi", [D, D], F32, "ExternalInput"),
             ("qsc", [64, 16], F32, "ExternalInput"),
             ("ksc", [64, 16], F32, "ExternalInput"),
             ("rc", [64, W0], BF16, "ExternalInput"),
             ("rssq", [64, W0], BF16, "ExternalInput"),
             ("rci", [64, W0], BF16, "ExternalInput"),
             ("rssi", [64, W0], BF16, "ExternalInput"),
             ("hinv", [1, W0], BF16, "ExternalInput"),
             ("yo", [D, T], F32, "ExternalOutput")]
    nc, ctx, dr = _phase_nc(specs)
    C = setup_common(ctx, with_xn=False, with_gg=False)
    YF = ctx.scratch("YF", [128, NCH * W0], F32)
    YB = ctx.scratch("YB", [128, NCH * W0], BF16)
    ctx._scr0 = ctx._scr
    y3 = dr["yown"].rearrange("(c p) t -> p c t", p=128)
    h3 = dr["yhalo"].rearrange("(c p) t -> p c t", p=128)
    YF3 = YF[:].rearrange("p (c t) -> p c t", c=NCH)
    ctx.g.dma_start(out=YF3[:, :, 258:W0], in_=y3, inc=ctx.gdma, inc_by=16)
    ctx.g.dma_start(out=YF3[:, :, 0:258], in_=h3, inc=ctx.gdma, inc_by=16)
    for e in ENGS:
        ctx.eng[e].wait(ctx.gdma)
    vS0 = ctx.sem("yb0")
    for c in range(NCH):
        ctx.v.tensor_copy(YB[:, c * W0:(c + 1) * W0], YF[:, c * W0:(c + 1) * W0],
                          inc=vS0)
    for e in ENGS:
        ctx.eng[e].wait(vS0)
    ctx.barrier(ENGS)
    emit_ffn0_attn(ctx, C, dr, YF, YB)
    ctx.barrier(ENGS)
    yo3 = dr["yo"].rearrange("(c p) t -> p c t", p=128)
    ctx.y.dma_start(out=yo3, in_=YF3[:, :, 258:W0], inc=ctx.sdma, inc_by=16)
    ctx.y.wait(ctx.sdma)
    return ctx.finalize()


def build_ffn0f_phase(last=False):
    W1 = 2049
    specs = [("yown", [D, T], F32, "ExternalInput"),
             ("ycol", [D, 1], F32, "ExternalInput"),
             ("w1a", [D, 2816], F32, "ExternalInput"),
             ("w1g", [D, 2816], F32, "ExternalInput"),
             ("w2p", [2816, D], F32, "ExternalInput"),
             ("b1a", [128, 22], F32, "ExternalInput"),
             ("b1g", [128, 22], F32, "ExternalInput"),
             ("lngi", [128, NCH], F32, "ExternalInput"),
             ("lnbi", [128, NCH], F32, "ExternalInput"),
             ("b2i", [128, NCH], F32, "ExternalInput")]
    specs.append(("gg", [D, T], BF16, "ExternalOutput") if last
                 else ("yo", [D, T], F32, "ExternalOutput"))
    nc, ctx, dr = _phase_nc(specs)
    C = setup_common(ctx, with_xn=False, with_gg=last)
    YF2 = ctx.scratch("YF2", [128, NCH * W1], F32)
    ctx._scr0 = ctx._scr
    y3 = dr["yown"].rearrange("(c p) t -> p c t", p=128)
    c3 = dr["ycol"].rearrange("(c p) t -> p c t", p=128)
    YF23 = YF2[:].rearrange("p (c t) -> p c t", c=NCH)
    ctx.y.dma_start(out=YF23[:, :, 1:W1], in_=y3, inc=ctx.sdma, inc_by=16)
    ctx.y.dma_start(out=YF23[:, :, 0:1], in_=c3, inc=ctx.sdma, inc_by=16,
                    allow_slow_non_contiguous=True)
    for e in ENGS:
        ctx.eng[e].wait(ctx.sdma)
    ctx.barrier(ENGS)
    emit_ffn0_ffn(ctx, C, dr, YF2, last)
    ctx.barrier(ENGS)
    if last:
        emit_store_fm(ctx, dr["gg"], C["GG"])
    else:
        yo3 = dr["yo"].rearrange("(c p) t -> p c t", p=128)
        ctx.y.dma_start(out=yo3, in_=YF23[:, :, 1:W1], inc=ctx.sdma, inc_by=16)
    ctx.y.wait(ctx.sdma)
    return ctx.finalize()


_BF = ml_dtypes.bfloat16

def _shard_fm(full, c, dtype=np.float32):
    b, h = c // 2, c % 2
    a = np.ascontiguousarray(full[b, h * 2048:(h + 1) * 2048, :].T)
    return a.astype(_BF) if dtype == "bf16" else a.astype(dtype)

def _shard_half(full, c, dtype="bf16"):
    b, h = c // 2, c % 2
    a = np.ascontiguousarray(full[b, (1 - h) * 2048:(2 - h) * 2048, :].T)
    return a.astype(_BF) if dtype == "bf16" else a.astype(np.float32)

def _unshard_fm(res, key):
    F = res[0][key].shape[0]
    out = np.zeros((4, 4096, F), np.float32)
    for c in range(8):
        b, h = c // 2, c % 2
        out[b, h * 2048:(h + 1) * 2048, :] = res[c][key].astype(np.float32).T
    return out

_PROGS = {}

def _prog(name, builder, *a, **k):
    if name not in _PROGS:
        _PROGS[name] = builder(*a, **k)
    return _PROGS[name]

def _run(nc, maps):
    return run_bass_kernel_spmd(nc, maps, core_ids=list(range(8))).results

def _attn3_maps_k(I, x, xn, gg):
    wq = np.ascontiguousarray(I["L3_attn_w_qkv"], np.float32)
    wo = np.ascontiguousarray(I["L3_attn_w_out"], np.float32)
    bo = np.ascontiguousarray(np.asarray(I["L3_attn_b_out"], np.float32).reshape(8, 128).T)
    lg = np.ascontiguousarray(np.asarray(I["L3_attn_ln_gamma"], np.float32).reshape(8, 128).T)
    lb = np.ascontiguousarray(np.asarray(I["L3_attn_ln_beta"], np.float32).reshape(8, 128).T)
    return [{"x": _shard_fm(x, c), "xn": _shard_fm(xn, c, "bf16"),
             "pxn": _shard_half(xn, c), "gg": _shard_fm(gg, c, "bf16"),
             "wqkv": wq, "wout": wo, "bout": bo, "lng": lg, "lnb": lb}
            for c in range(8)]

def _ffn3_maps_k(I, xn):
    wfc = np.ascontiguousarray(np.asarray(I["L3_ffn_W_fc"], np.float32).T)
    wpj = np.ascontiguousarray(np.asarray(I["L3_ffn_W_proj"], np.float32).T)
    lnw = np.ascontiguousarray(np.asarray(I["L3_ffn_ln_w"], np.float32).reshape(32, 128).T)
    lnb = np.ascontiguousarray(np.asarray(I["L3_ffn_ln_b"], np.float32).reshape(32, 128).T)
    return [{"xn": _shard_fm(xn, c, "bf16"), "wfc": wfc, "wproj": wpj,
             "lnw": lnw, "lnb": lnb} for c in range(8)]

def _attn2_maps_k(I, x, xn, gg):
    wq = np.ascontiguousarray(I["L2_attn_Wq"], np.float32)
    bq = np.ascontiguousarray(np.asarray(I["L2_attn_bq"], np.float32).reshape(8, 128).T)
    wk = np.ascontiguousarray(I["L2_attn_Wk"], np.float32)
    wv = np.ascontiguousarray(I["L2_attn_Wv"], np.float32)
    wo = np.ascontiguousarray(I["L2_attn_Wo"], np.float32)
    bo = np.ascontiguousarray(np.asarray(I["L2_attn_bo"], np.float32).reshape(8, 128).T)
    maps = []
    for c in range(8):
        b, h = c // 2, c % 2
        own = np.arange(h * 2048, (h + 1) * 2048, dtype=np.float32)
        par = np.arange((1 - h) * 2048, (2 - h) * 2048, dtype=np.float32)
        kpos = np.concatenate([own, par])
        slopes = (2.0 ** (-8.0 / 16)) ** np.arange(1, 17, dtype=np.float32)
        alib = np.zeros((128, 512), np.float32)
        for hh in range(16):
            for kc in range(32):
                alib[:, hh * 32 + kc] = slopes[hh] * kpos[kc * 128:(kc + 1) * 128]
        pmask = np.zeros((1, 4096), _BF)
        if h == 0:
            pmask[0, 2048:] = -1e9
        qalibi = (-slopes[:, None] * 8.0 * own[None, :]).astype(_BF)
        maps.append({"x": _shard_fm(x, c), "xn": _shard_fm(xn, c, "bf16"),
                     "pxn": _shard_half(xn, c), "gg": _shard_fm(gg, c, "bf16"),
                     "wq": wq, "bq": bq, "wk": wk, "wv": wv, "wo": wo, "bo": bo,
                     "alib": alib, "qalibi": qalibi, "pmask": pmask})
    return maps

def _ffn2_maps_k(I, xn):
    rwt = np.ascontiguousarray(np.asarray(I["L2_ffn_router_w"], np.float32).T)
    rbb = np.ascontiguousarray(np.asarray(I["L2_ffn_router_b"], np.float32).reshape(1, 16)).astype(_BF)
    ew = np.ascontiguousarray(np.asarray(I["L2_ffn_expert_w"], np.float32).reshape(16 * 1024, 1024))
    esel = np.zeros((17, 17 * 128), _BF)
    for e in range(17):
        esel[e, e * 128:(e + 1) * 128] = 1
    eye = np.eye(128).astype(_BF)
    return [{"xn": _shard_fm(xn, c, "bf16"), "rwt": rwt, "rbb": rbb, "ew": ew,
             "esel": esel, "eye": eye} for c in range(8)]

def _attn1_maps_k(I, x, xn, gg):
    wqkv = np.asarray(I["L1_attn_W_qkv"], np.float32)
    bqkv = np.asarray(I["L1_attn_b_qkv"], np.float32)
    wp = np.ascontiguousarray(np.asarray(I["L1_attn_W_proj"], np.float32).T)
    bp = np.ascontiguousarray(np.asarray(I["L1_attn_b_proj"], np.float32).reshape(8, 128).T)
    gm = np.ascontiguousarray(np.asarray(I["L1_attn_gamma"], np.float32).reshape(8, 128).T)
    bt = np.ascontiguousarray(np.asarray(I["L1_attn_beta"], np.float32).reshape(8, 128).T)
    maps = []
    for c in range(8):
        b, h = c // 2, c % 2
        hs = slice(h * 512, (h + 1) * 512)
        wq1 = np.ascontiguousarray(wqkv[0:1024][hs].T)
        wk1 = np.ascontiguousarray(wqkv[1024:2048][hs].T)
        wv1 = np.ascontiguousarray(wqkv[2048:3072][hs].T)
        bq1 = np.ascontiguousarray(bqkv[0:1024][hs].reshape(4, 128).T)
        bk1 = np.ascontiguousarray(bqkv[1024:2048][hs].reshape(4, 128).T)
        bv1 = np.ascontiguousarray(bqkv[2048:3072][hs].reshape(4, 128).T)
        maps.append({"x": _shard_fm(x, c), "xn": _shard_fm(xn, c, "bf16"),
                     "pxn": _shard_half(xn, c), "gg": _shard_fm(gg, c, "bf16"),
                     "wq1": wq1, "wk1": wk1, "wv1": wv1,
                     "bq1": bq1, "bk1": bk1, "bv1": bv1,
                     "wproj1": wp, "bproj": bp, "gamma1": gm, "beta1": bt})
    return maps

def _ffn1_maps_k(I, xn):
    cm = lambda a: np.ascontiguousarray(np.asarray(a, np.float32).reshape(-1, 128).T)
    m = {"w1t": np.ascontiguousarray(np.asarray(I["L1_ffn_w1_w"], np.float32).T),
         "w2t": np.ascontiguousarray(np.asarray(I["L1_ffn_w2_w"], np.float32).T),
         "h1t": np.ascontiguousarray(np.asarray(I["L1_ffn_h1_w"], np.float32).T),
         "h2t": np.ascontiguousarray(np.asarray(I["L1_ffn_h2_w"], np.float32).T),
         "w3t": np.ascontiguousarray(np.asarray(I["L1_ffn_w3_w"], np.float32).T),
         "s1c": cm(I["L1_ffn_w1_s"]), "s2c": cm(I["L1_ffn_w2_s"]),
         "h1sc": cm(I["L1_ffn_h1_s"]), "h2sc": cm(I["L1_ffn_h2_s"]),
         "w3sc": cm(I["L1_ffn_w3_s"])}
    return [dict(m, xn=_shard_fm(xn, c, "bf16")) for c in range(8)]

def _attn0_maps_k(I, x, xn, gg):
    p = lambda n: np.asarray(I["L0_attn_" + n], np.float32)
    wu3 = np.zeros((33, 8 * 384), np.float32)
    for h in range(8):
        wu3[0:32, h * 384:h * 384 + 128] = p("W_UQ")[:, h * 128:(h + 1) * 128]
        wu3[0:32, h * 384 + 128:h * 384 + 256] = p("W_UK")[:, h * 128:(h + 1) * 128]
        wu3[0:32, h * 384 + 256:h * 384 + 384] = p("W_UV")[:, h * 128:(h + 1) * 128]
        wu3[32, h * 384 + 256:h * 384 + 384] = p("b_UV")[h * 128:(h + 1) * 128]
    bsm = np.zeros((128, 3), np.float32)
    bsm[0:32, 0] = p("b_DQ"); bsm[0:32, 1] = p("b_DKV"); bsm[0:32, 2] = p("b_KR")
    m = {"wdq": np.ascontiguousarray(p("W_DQ")), "wdkv": np.ascontiguousarray(p("W_DKV")),
         "wkr": np.ascontiguousarray(p("W_KR")), "bsm": bsm, "wu3": wu3,
         "wqr": np.ascontiguousarray(p("W_QR")),
         "buq": np.ascontiguousarray(p("b_UQ").reshape(8, 128).T),
         "buk": np.ascontiguousarray(p("b_UK").reshape(8, 128).T),
         "bqr": np.ascontiguousarray(p("b_QR").reshape(2, 128).T),
         "wo": np.ascontiguousarray(p("W_O")),
         "bo": np.ascontiguousarray(p("b_O").reshape(8, 128).T)}
    return [dict(m, x=_shard_fm(x, c), xn=_shard_fm(xn, c, "bf16"),
                 pxn=_shard_half(xn, c), gg=_shard_fm(gg, c, "bf16"))
            for c in range(8)]

def _rot_tables_k(core):
    h = core % 2
    own0 = h * 2048
    tglob = np.arange(own0 - 258, own0 + 2048)
    tglob = np.maximum(tglob, 0)
    half = np.arange(0, 64, 2, dtype=np.float32)
    inv_freq = 1.0 / 10000.0 ** (half / 64)
    freqs = tglob[:, None] * inv_freq[None, :]
    base_scale = (half + 25.6) / 89.6
    power = (tglob - 2048) / 128.0
    scale = base_scale[None, :] ** power[:, None]
    cs = (np.cos(freqs) * scale).T
    ss = (np.sin(freqs) * scale).T
    csi = (np.cos(freqs) / scale).T
    ssi = (np.sin(freqs) / scale).T
    mk = lambda a_, b_: np.ascontiguousarray(np.concatenate([a_, b_], 0)).astype(_BF)
    return mk(cs, cs), mk(ss, -ss), mk(csi, csi), mk(ssi, -ssi)

def _ffn0a_maps_k(I, i, yfull, first=False):
    wqkv = np.asarray(I["L0_ffn_Wqkv"][i], np.float32)
    wq3 = np.zeros((1024, 16 * 192), np.float32)
    for hh in range(16):
        wq3[:, hh * 192:hh * 192 + 64] = wqkv[:, hh * 64:(hh + 1) * 64]
        wq3[:, hh * 192 + 64:hh * 192 + 128] = wqkv[:, 1024 + hh * 64:1024 + (hh + 1) * 64]
        wq3[:, hh * 192 + 128:hh * 192 + 192] = wqkv[:, 2048 + hh * 64:2048 + (hh + 1) * 64]
    qs = np.tile(np.asarray(I["L0_ffn_q_scale"][i], np.float32)[:, None], (1, 16))
    ks = np.tile(np.asarray(I["L0_ffn_k_scale"][i], np.float32)[:, None], (1, 16))
    wo = np.ascontiguousarray(np.asarray(I["L0_ffn_Wo"][i], np.float32))
    maps = []
    dt_ = "bf16" if first else np.float32
    for c in range(8):
        b, h = c // 2, c % 2
        rc, rssq, rci, rssi = _rot_tables_k(c)
        hinv = np.zeros((1, 2306), _BF)
        if h == 0:
            yh = np.zeros((1024, 258), np.float32)
            hinv[0, 0:258] = -1e9
        else:
            yh = np.ascontiguousarray(yfull[b, 2048 - 258:2048].T.astype(np.float32))
        if first:
            yh = yh.astype(_BF) if h == 1 else np.zeros((1024, 258), _BF)
        maps.append({"yown": _shard_fm(yfull, c, dt_),
                     "yhalo": np.ascontiguousarray(yh),
                     "wq3i": wq3, "woi": wo, "qsc": qs, "ksc": ks,
                     "rc": rc, "rssq": rssq, "rci": rci, "rssi": rssi,
                     "hinv": hinv})
    return maps

def _ffn0f_maps_k(I, i, ymid, last=False):
    cm = lambda a: np.ascontiguousarray(np.asarray(a, np.float32).reshape(-1, 128).T)
    w1 = np.asarray(I["L0_ffn_W1"][i], np.float32)
    b1 = np.asarray(I["L0_ffn_b1"][i], np.float32)
    w1a = np.zeros((1024, 2816), np.float32); w1a[:, :2730] = w1[:, :2730]
    w1g = np.zeros((1024, 2816), np.float32); w1g[:, :2730] = w1[:, 2730:]
    b1a = np.zeros(2816, np.float32); b1a[:2730] = b1[:2730]
    b1g = np.zeros(2816, np.float32); b1g[:2730] = b1[2730:]
    w2p = np.zeros((2816, 1024), np.float32)
    w2p[:2730] = np.asarray(I["L0_ffn_W2"][i], np.float32)
    m = {"w1a": w1a, "w1g": w1g, "w2p": w2p, "b1a": cm(b1a), "b1g": cm(b1g),
         "lngi": cm(I["L0_ffn_ln_g"][i]), "lnbi": cm(I["L0_ffn_ln_b"][i]),
         "b2i": cm(I["L0_ffn_b2"][i])}
    maps = []
    for c in range(8):
        b, h = c // 2, c % 2
        if h == 0:
            yc = np.zeros((1024, 1), np.float32)
        else:
            yc = np.ascontiguousarray(ymid[b, 2047:2048].T.astype(np.float32))
        maps.append(dict(m, yown=_shard_fm(ymid, c), ycol=yc))
    return maps

def kernel(**inputs):
    I = inputs
    if USE_FUSED:
        x = np.asarray(I["L0_attn_h_t"], np.float32)
        nc = _prog("fused", build_fused)
        return _unshard_fm(_run(nc, _fused_maps(I, x)), "xo")
    return kernel_unfused(I)


def _old_kernel_unused(**inputs):
    I = inputs
    x = np.asarray(I["L0_attn_h_t"], np.float32)
    for layer in range(4):
        nc_ln = _prog("ln", build_ln_phase)
        xn = _unshard_fm(_run(nc_ln, [{"x": _shard_fm(x, c)} for c in range(8)]),
                         "xn")
        if layer == 0:
            y = xn
            for i in range(6):
                nca = _prog("ffn0a_f" if i == 0 else "ffn0a",
                            build_ffn0a_phase, first=(i == 0))
                ymid = _unshard_fm(_run(nca, _ffn0a_maps_k(I, i, y, first=(i == 0))),
                                   "yo")
                ncf = _prog("ffn0f_l" if i == 5 else "ffn0f",
                            build_ffn0f_phase, last=(i == 5))
                y = _unshard_fm(_run(ncf, _ffn0f_maps_k(I, i, ymid, last=(i == 5))),
                                "gg" if i == 5 else "yo")
            gg = y
            nc_at = _prog("attn0", build_attn0_phase)
            x = _unshard_fm(_run(nc_at, _attn0_maps_k(I, x, xn, gg)), "xo")
        elif layer == 1:
            gg = _unshard_fm(_run(_prog("ffn1", build_ffn1_phase),
                                  _ffn1_maps_k(I, xn)), "gg")
            x = _unshard_fm(_run(_prog("attn1", build_attn1_phase),
                                 _attn1_maps_k(I, x, xn, gg)), "xo")
        elif layer == 2:
            gg = _unshard_fm(_run(_prog("ffn2", build_ffn2_phase),
                                  _ffn2_maps_k(I, xn)), "gg")
            x = _unshard_fm(_run(_prog("attn2", build_attn2_phase),
                                 _attn2_maps_k(I, x, xn, gg)), "xo")
        else:
            gg = _unshard_fm(_run(_prog("ffn3", build_ffn3_phase),
                                  _ffn3_maps_k(I, xn)), "gg")
            x = _unshard_fm(_run(_prog("attn3", build_attn3_phase),
                                 _attn3_maps_k(I, x, xn, gg)), "xo")
    return x


_WSPEC = {
    "a3": [("wqkv", [D, 3 * D]), ("wout", [D, D]), ("bout", [128, NCH]),
           ("lng", [128, NCH]), ("lnb", [128, NCH])],
    "f3": [("wfc", [D, 4096]), ("wproj", [4096, D]), ("lnw", [128, 32]),
           ("lnb", [128, 32])],
    "a2": [("wq", [D, D]), ("bq", [128, NCH]), ("wk", [D, D]), ("wv", [D, D]),
           ("wo", [D, D]), ("bo", [128, NCH]), ("alib", [128, 512])],
    "f2": [("rwt", [D, 16]), ("ew", [16 * D, D])],
    "a1": [("wq1", [D, 512]), ("wk1", [D, 512]), ("wv1", [D, 512]),
           ("bq1", [128, 4]), ("bk1", [128, 4]), ("bv1", [128, 4]),
           ("wproj1", [D, D]), ("bproj", [128, NCH]), ("gamma1", [128, NCH]),
           ("beta1", [128, NCH])],
    "f1": [("w1t", [D, 4096]), ("w2t", [D, 4096]), ("h1t", [4096, 4096]),
           ("h2t", [4096, 4096]), ("w3t", [4096, D]), ("s1c", [128, 32]),
           ("s2c", [128, 32]), ("h1sc", [128, 32]), ("h2sc", [128, 32]),
           ("w3sc", [128, NCH])],
    "a0": [("wdq", [D, 32]), ("wdkv", [D, 32]), ("wkr", [D, 32]),
           ("bsm", [128, 3]), ("wu3", [33, NCH * 384]), ("wqr", [32, 256]),
           ("buq", [128, NCH]), ("buk", [128, NCH]), ("bqr", [128, 2]),
           ("wo", [D, D]), ("bo", [128, NCH])],
}
for _i in range(6):
    _WSPEC[f"fa{_i}"] = [("wq3i", [D, 16 * 192]), ("woi", [D, D]),
                         ("qsc", [64, 16]), ("ksc", [64, 16])]
    _WSPEC[f"ff{_i}"] = [("w1a", [D, 2816]), ("w1g", [D, 2816]),
                         ("w2p", [2816, D]), ("b1a", [128, 22]),
                         ("b1g", [128, 22]), ("lngi", [128, NCH]),
                         ("lnbi", [128, NCH]), ("b2i", [128, NCH])]
_WBF16 = {("a2", "qalibi"), ("a2", "pmask"), ("f2", "rbb"), ("f2", "esel"),
          ("f2", "eye"), ("f0c", "rc"), ("f0c", "rssq"), ("f0c", "rci"),
          ("f0c", "rssi"), ("f0c", "hinv")}
_WSPEC["a2"] += [("qalibi", [16, T]), ("pmask", [1, S])]
_WSPEC["f2"] += [("rbb", [1, 16]), ("esel", [17, 17 * 128]), ("eye", [128, 128])]
_WSPEC["f0c"] = [("rc", [64, W0]), ("rssq", [64, W0]), ("rci", [64, W0]),
                 ("rssi", [64, W0]), ("hinv", [1, W0])]


def build_fused(comm="dev", twins=True):
    nc = bass.Bass("TRN2", target_bir_lowering=False,
                   detect_race_conditions=False)
    ctx = Ctx(nc)
    ext = lambda n, sh, dt_: nc.dram_tensor(n, list(sh), dt_,
                                            kind="ExternalInput").ap()
    intd = lambda n, sh, dt_: nc.dram_tensor(n, list(sh), dt_,
                                             kind="Internal").ap()
    xin = ext("x", [D, T], F32)
    xout = nc.dram_tensor("xo", [D, T], F32, kind="ExternalOutput").ap()
    poff_in = ext("poff", [1, 1], mybir.dt.int32)
    hsc_in = ext("hscale", [128, 1], F32)
    xD = intd("xD", [D, T], F32)
    xnD = intd("xnD", [D, T], BF16)
    ggD = intd("ggD", [D, T], BF16)
    yD = intd("yD", [D, T], F32)
    ymD = intd("ymD", [D, T], F32)
    if comm == "dev":
        xgD = intd("xgD", [2 * D, T], BF16)
        pxnD = intd("pxnD", [D, T], BF16)
        tsb = intd("tsb", [D, 258], BF16)
        tgb = intd("tgb", [2 * D, 258], BF16)
        tsf = intd("tsf", [D, 258], F32)
        tgf = intd("tgf", [2 * D, 258], F32)
        csf = intd("csf", [D, 1], F32)
        cgf = intd("cgf", [2 * D, 1], F32)
    else:
        pxnL = {L: ext(f"simpxn{L}", [D, T], BF16) for L in range(4)}
        tgb = ext("simtgb", [2 * D, 258], BF16)
        tgf = ext("simtgf", [2 * D, 258], F32)
        cgf = ext("simcgf", [2 * D, 1], F32)
    DRW = {}
    for pfx, items in _WSPEC.items():
        DRW[pfx] = {k: ext(f"{pfx}__{k}",
                           sh, BF16 if (pfx, k) in _WBF16 else F32)
                    for k, sh in items}
    hsp = intd("hsp", [4096, T], BF16)
    wpq = intd("wpq", [4096, D], BF16)
    hsp1 = intd("hsp1", [4096, T], BF16)
    onesb = ctx.fixed("ones128b", [128, 1], BF16)
    onesf = ctx.fixed("ones128f", [128, 1], F32)
    ones1f = ctx.fixed("ones1x128f", [1, 128], F32)
    ones1b = ctx.fixed("ones1x128b", [1, 128], BF16)
    POFFC = ctx.CONSTS[:, 63:64].bitcast(mybir.dt.int32)
    HSC = ctx.CONSTS[:, 62:63]
    ctx.end_fixed()
    ctx.g.memset(onesb[:], 1.0)
    ctx.g.memset(onesf[:], 1.0)
    ctx.g.memset(ones1f[:], 1.0)
    ctx.g.memset(ones1b[:], 1.0)
    ctx.g.dma_start(out=POFFC[0:1, 0:1], in_=poff_in, inc=ctx.gdma, inc_by=16)
    ctx.g.dma_start(out=HSC[0:128, 0:1], in_=hsc_in, inc=ctx.gdma, inc_by=16)
    ctx.constcol(1.0 / math.sqrt(32.0))
    for e in ENGS:
        ctx.eng[e].wait(ctx.gdma)
    ctx.barrier(ENGS)
    C0 = {"ones128_bf": onesb[:], "ones128_f32": onesf[:],
          "ones1x128_f32": ones1f[:], "ones1x128_bf": ones1b[:],
          "XN": None, "GG": None}
    ccS = ctx.sem("ccS")
    RG = [[0, 1], [2, 3], [4, 5], [6, 7]]

    def collect(src, dst):
        ctx.g.wait(ctx.sdma)
        ctx.g.wait(ctx.gdma)
        ctx.g.collective_compute("AllGather", ALU.bypass, replica_groups=RG,
                                 ins=[src.opt()], outs=[dst.opt()], inc=ccS)
        ctx.g.wait(ccS)

    def dyncopy_pxn():
        ctx.gdma.n += 16
        h = ctx.gdma.h
        def f(E):
            with E.register(ctx.uname("pr")) as r:
                E.reg_load(r, POFFC[0:1, 0:1])
                off = E.snap(r)
                E.dma_start(out=pxnD,
                            in_=xgD[bass.ds(off, D), :]).then_inc(h, 16)
        ctx.g.raw(f)
        ctx.last_inc["gpsimd"] = (ctx.gdma, ctx.gdma.n)
        ctx.g.wait(ctx.gdma)

    PRECAST = {} if not twins else {
               **{f"fa{i}": [(f"fa{i}", "wq3i"), (f"fa{i}", "woi")]
                  for i in range(6)},
               **{f"ff{i}": [(f"ff{i}", "w1a"), (f"ff{i}", "w1g"),
                             (f"ff{i}", "w2p")] for i in range(6)},
               "f1a": [("f1", "w1t"), ("f1", "w2t")], "f1b": [("f1", "h1t")],
               "f1c": [("f1", "h2t")], "f1d": [("f1", "w3t")],
               "f2": [("f2", "ew")]}
    BFW, pre_tgt = {}, {}
    preS = ctx.sem("preS")
    for label, items in PRECAST.items():
        for pfx, k in items:
            shd = dict(_WSPEC[pfx])
            BFW[(pfx, k)] = intd(f"{pfx}_{k}_bf", shd[k], BF16)

    def precast(label):
        if label not in PRECAST:
            return
        li = ctx.last_inc["gpsimd"]
        for pfx, k in PRECAST[label]:
            ctx.g.dma_start(out=BFW[(pfx, k)], in_=DRW[pfx][k], inc=preS,
                            inc_by=16)
        pre_tgt[label] = preS.n
        ctx.last_inc["gpsimd"] = li

    def pre_wait(pfx):
        if pfx not in pre_tgt:
            return
        for e in ENGS:
            ctx.eng[e].wait(preS, pre_tgt[pfx])

    def drx(pfx):
        return {k: BFW.get((pfx, k), v) for k, v in DRW[pfx].items()}

    def halo_scale(buf, stride, w):
        for c in range(NCH):
            ctx.v.tensor_single_scalar(buf[:, c * stride:c * stride + w],
                                       buf[:, c * stride:c * stride + w],
                                       HSC[:, 0:1], ALU.mult, inc=ctx.sem("hsv"))
        ctx.v.wait(ctx.sem("hsv"))

    def phase_ln(xsrc):
        pm = ctx.scratch_mark()
        XN = ctx.scratch("pXN", [128, NCH * T], BF16)
        XRES = ctx.scratch("pXRES", [128, NCH * T], F32)
        C = dict(C0, XN=XN)
        emit_load_fm(ctx, XRES, xsrc)
        for e in ("tensor", "vector", "scalar"):
            ctx.eng[e].wait(ctx.sdma)
        ctx.barrier(("gpsimd", "vector", "scalar", "tensor"))
        emit_ln(ctx, C, XRES, XN)
        ctx.barrier(ENGS)
        emit_store_fm(ctx, xnD, XN)
        ctx.y.wait(ctx.sdma)
        ctx.barrier(ENGS)
        ctx.scratch_release(pm)

    def phase_ffn_simple(emitfn, dr, extras=()):
        pm = ctx.scratch_mark()
        XN = ctx.scratch("pXN", [128, NCH * T], BF16)
        GG = ctx.scratch("pGG", [128, NCH * T], BF16)
        C = dict(C0, XN=XN, GG=GG)
        emit_load_fm(ctx, XN, xnD)
        for e in ENGS:
            ctx.eng[e].wait(ctx.sdma)
        ctx.barrier(ENGS)
        emitfn(ctx, C, dr, *extras)
        ctx.barrier(ENGS)
        emit_store_fm(ctx, ggD, GG)
        ctx.y.wait(ctx.sdma)
        ctx.barrier(ENGS)
        ctx.scratch_release(pm)

    def phase_attn(kind, dr, pxn_src, xsrc, xdst):
        pm = ctx.scratch_mark()
        XN = ctx.scratch("pXN", [128, NCH * T], BF16)
        GG = ctx.scratch("pGG", [128, NCH * T], BF16)
        OSB = ctx.scratch("pOSB", [128, NCH * T], BF16)
        C = dict(C0, XN=XN, GG=GG)
        mark_pxn = ctx.scratch_mark()
        PXN = ctx.scratch("pPXN", [128, NCH * T], BF16)
        emit_load_fm(ctx, XN, xnD)
        emit_load_fm(ctx, PXN, pxn_src)
        emit_load_fm(ctx, GG, ggD)
        for e in ENGS:
            ctx.eng[e].wait(ctx.sdma)
        ctx.barrier(ENGS)
        if kind == 3:
            emit_attn_core(ctx, C, XN, PXN, OSB, dr["wqkv"], 32, 32,
                           1.0 / math.sqrt(32.0), D, 2 * D)
        elif kind == 2:
            emit_attn2(ctx, C, XN, PXN, OSB, dr)
        elif kind == 1:
            emit_attn1(ctx, C, XN, PXN, OSB, dr)
        else:
            emit_attn0(ctx, C, XN, PXN, OSB, dr)
        ctx.barrier(ENGS)
        ctx.scratch_release(mark_pxn)
        if kind == 1:
            emit_l1_combine(ctx, C, dr, OSB, xsrc, xdst)
            ctx.barrier(ENGS)
            ctx.y.wait(ctx.gdma)
        elif kind == 3:
            emit_l3_combine(ctx, C, dict(dr, x=xsrc), OSB)
            ctx.barrier(ENGS)
            emit_store_fm(ctx, xdst, ctx._xres_out)
        else:
            XRES = ctx.scratch("pXRES", [128, NCH * T], F32)
            emit_load_fm(ctx, XRES, xsrc)
            for e in ("vector", "scalar"):
                ctx.eng[e].wait(ctx.sdma)
            emit_combine(ctx, C, dr["wo"], dr["bo"], OSB, GG, XRES)
            ctx.barrier(ENGS)
            emit_store_fm(ctx, xdst, XRES)
        ctx.y.wait(ctx.sdma)
        ctx.barrier(ENGS)
        ctx.scratch_release(pm)

    def phase_f0a(i, post_cast=()):
        dr = dict(drx(f"fa{i}"), **DRW["f0c"])
        if comm == "dev":
            if i == 0:
                ctx.g.dma_start(out=tsb, in_=xnD[:, T - 258:T],
                                inc=ctx.gdma, inc_by=16)
                collect(tsb, tgb)
                halo, halo_bf = tgb[0:D, :], True
            else:
                ctx.g.dma_start(out=tsf, in_=yD[:, T - 258:T],
                                inc=ctx.gdma, inc_by=16)
                collect(tsf, tgf)
                halo, halo_bf = tgf[0:D, :], False
        else:
            halo, halo_bf = (tgb[0:D, :], True) if i == 0 else (tgf[0:D, :], False)
        pm = ctx.scratch_mark()
        YF = ctx.scratch("pYF", [128, NCH * W0], F32)
        YB = ctx.scratch("pYB", [128, NCH * W0], BF16)
        C = dict(C0)
        ysrc = xnD if i == 0 else yD
        y3 = ysrc.rearrange("(c p) t -> p c t", p=128)
        h3 = halo.rearrange("(c p) t -> p c t", p=128)
        YF3 = YF[:].rearrange("p (c t) -> p c t", c=NCH)
        q = ctx.g if i == 0 else ctx.y
        qs = ctx.gdma if i == 0 else ctx.sdma
        q.dma_start(out=YF3[:, :, 258:W0], in_=y3, inc=qs, inc_by=16)
        if halo_bf:
            ctx.g.dma_start(out=YF3[:, :, 0:258], in_=h3, inc=ctx.gdma,
                            inc_by=16)
        else:
            ctx.y.dma_start(out=YF3[:, :, 0:258], in_=h3, inc=ctx.sdma,
                            inc_by=16)
        for pfx in post_cast:
            precast(pfx)
        ctx.v.wait(ctx.gdma)
        ctx.v.wait(ctx.sdma)
        halo_scale(YF, W0, 258)
        for e in ENGS:
            ctx.eng[e].wait(ctx.gdma)
            ctx.eng[e].wait(ctx.sdma)
            ctx.eng[e].wait(ctx.sem("hsv"))
        vS0 = ctx.sem("yb0")
        for c in range(NCH):
            ctx.v.tensor_copy(YB[:, c * W0:(c + 1) * W0],
                              YF[:, c * W0:(c + 1) * W0], inc=vS0)
        for e in ENGS:
            ctx.eng[e].wait(vS0)
        ctx.barrier(ENGS)
        emit_ffn0_attn(ctx, C, dr, YF, YB)
        ctx.barrier(ENGS)
        ym3 = ymD.rearrange("(c p) t -> p c t", p=128)
        ctx.y.dma_start(out=ym3, in_=YF3[:, :, 258:W0], inc=ctx.sdma, inc_by=16)
        ctx.y.wait(ctx.sdma)
        ctx.barrier(ENGS)
        ctx.scratch_release(pm)

    def phase_f0f(i):
        dr = drx(f"ff{i}")
        last = (i == 5)
        if comm == "dev":
            ctx.y.dma_start(out=csf, in_=ymD[:, T - 1:T], inc=ctx.sdma,
                            inc_by=16, allow_slow_non_contiguous=True)
            collect(csf, cgf)
        W1 = 2049
        pm = ctx.scratch_mark()
        YF2 = ctx.scratch("pYF2", [128, NCH * W1], F32)
        GG = ctx.scratch("pGG", [128, NCH * T], BF16) if last else None
        C = dict(C0, GG=GG)
        y3 = ymD.rearrange("(c p) t -> p c t", p=128)
        c3 = cgf[0:D, :].rearrange("(c p) t -> p c t", p=128)
        YF23 = YF2[:].rearrange("p (c t) -> p c t", c=NCH)
        ctx.y.dma_start(out=YF23[:, :, 1:W1], in_=y3, inc=ctx.sdma, inc_by=16)
        ctx.y.dma_start(out=YF23[:, :, 0:1], in_=c3, inc=ctx.sdma, inc_by=16,
                        allow_slow_non_contiguous=True)
        ctx.v.wait(ctx.sdma)
        halo_scale(YF2, W1, 1)
        for e in ENGS:
            ctx.eng[e].wait(ctx.sdma)
            ctx.eng[e].wait(ctx.sem("hsv"))
        ctx.barrier(ENGS)
        emit_ffn0_ffn(ctx, C, dr, YF2, last)
        ctx.barrier(ENGS)
        if last:
            emit_store_fm(ctx, ggD, GG)
        else:
            yo3 = yD.rearrange("(c p) t -> p c t", p=128)
            ctx.y.dma_start(out=yo3, in_=YF23[:, :, 1:W1], inc=ctx.sdma,
                            inc_by=16)
        ctx.y.wait(ctx.sdma)
        ctx.barrier(ENGS)
        ctx.scratch_release(pm)

    precast("fa0")
    precast("ff0")
    for L in range(4):
        xsrc = xin if L == 0 else xD
        xdst = xout if L == 3 else xD
        phase_ln(xsrc)
        if comm == "dev":
            collect(xnD, xgD)
            dyncopy_pxn()
            pxn_src = pxnD
        else:
            pxn_src = pxnL[L]
        if L == 0:
            for i in range(6):
                nxt = ([f"fa{i+1}", f"ff{i+1}"] if i < 5 else [])
                if i >= 2:
                    nxt.append(["f1a", "f1b", "f1c", "f1d"][i - 2])
                pre_wait(f"fa{i}")
                phase_f0a(i, post_cast=nxt)
                pre_wait(f"ff{i}")
                phase_f0f(i)
            phase_attn(0, DRW["a0"], pxn_src, xsrc, xdst)
        elif L == 1:
            pre_wait("f1d")
            precast("f2")
            phase_ffn_simple(emit_ffn1, drx("f1"), (hsp1,))
            phase_attn(1, DRW["a1"], pxn_src, xsrc, xdst)
        elif L == 2:
            pre_wait("f2")
            phase_ffn_simple(emit_ffn2, drx("f2"))
            phase_attn(2, DRW["a2"], pxn_src, xsrc, xdst)
        else:
            phase_ffn_simple(emit_ffn3, DRW["f3"], (hsp, wpq))
            phase_attn(3, DRW["a3"], pxn_src, xsrc, xdst)
    return ctx.finalize()


def _f0a_w(I, i):
    wqkv = np.asarray(I["L0_ffn_Wqkv"][i], np.float32)
    wq3 = np.zeros((1024, 16 * 192), np.float32)
    for hh in range(16):
        wq3[:, hh * 192:hh * 192 + 64] = wqkv[:, hh * 64:(hh + 1) * 64]
        wq3[:, hh * 192 + 64:hh * 192 + 128] = wqkv[:, 1024 + hh * 64:1024 + (hh + 1) * 64]
        wq3[:, hh * 192 + 128:hh * 192 + 192] = wqkv[:, 2048 + hh * 64:2048 + (hh + 1) * 64]
    return {"wq3i": wq3,
            "woi": np.ascontiguousarray(np.asarray(I["L0_ffn_Wo"][i], np.float32)),
            "qsc": np.tile(np.asarray(I["L0_ffn_q_scale"][i], np.float32)[:, None], (1, 16)),
            "ksc": np.tile(np.asarray(I["L0_ffn_k_scale"][i], np.float32)[:, None], (1, 16))}


def _f0f_w(I, i):
    cm = lambda a: np.ascontiguousarray(np.asarray(a, np.float32).reshape(-1, 128).T)
    w1 = np.asarray(I["L0_ffn_W1"][i], np.float32)
    b1 = np.asarray(I["L0_ffn_b1"][i], np.float32)
    w1a = np.zeros((1024, 2816), np.float32); w1a[:, :2730] = w1[:, :2730]
    w1g = np.zeros((1024, 2816), np.float32); w1g[:, :2730] = w1[:, 2730:]
    b1a = np.zeros(2816, np.float32); b1a[:2730] = b1[:2730]
    b1g = np.zeros(2816, np.float32); b1g[:2730] = b1[2730:]
    w2p = np.zeros((2816, 1024), np.float32)
    w2p[:2730] = np.asarray(I["L0_ffn_W2"][i], np.float32)
    return {"w1a": w1a, "w1g": w1g, "w2p": w2p, "b1a": cm(b1a), "b1g": cm(b1g),
            "lngi": cm(I["L0_ffn_ln_g"][i]), "lnbi": cm(I["L0_ffn_ln_b"][i]),
            "b2i": cm(I["L0_ffn_b2"][i])}


def _fused_maps(I, x):
    a3 = _attn3_maps_k(I, x, x, x)
    f3 = _ffn3_maps_k(I, x)
    a2 = _attn2_maps_k(I, x, x, x)
    f2 = _ffn2_maps_k(I, x)
    a1 = _attn1_maps_k(I, x, x, x)
    f1 = _ffn1_maps_k(I, x)
    a0 = _attn0_maps_k(I, x, x, x)
    faw = [_f0a_w(I, i) for i in range(6)]
    ffw = [_f0f_w(I, i) for i in range(6)]
    skip = {"x", "xn", "pxn", "gg"}
    maps = []
    for c in range(8):
        b, h = c // 2, c % 2
        m = {"x": _shard_fm(x, c),
             "poff": np.full((1, 1), 1024 * (1 - h), np.int32),
             "hscale": np.full((128, 1), float(h), np.float32)}
        for pfx, percore in (("a3", a3), ("f3", f3), ("a2", a2), ("f2", f2),
                             ("a1", a1), ("f1", f1), ("a0", a0)):
            for k, v in percore[c].items():
                if k not in skip:
                    m[f"{pfx}__{k}"] = v
        for i in range(6):
            for k, v in faw[i].items():
                m[f"fa{i}__{k}"] = v
            for k, v in ffw[i].items():
                m[f"ff{i}__{k}"] = v
        rc, rssq, rci, rssi = _rot_tables_k(c)
        hinv = np.zeros((1, 2306), _BF)
        if h == 0:
            hinv[0, 0:258] = -1e9
        m["f0c__rc"] = rc; m["f0c__rssq"] = rssq
        m["f0c__rci"] = rci; m["f0c__rssi"] = rssi; m["f0c__hinv"] = hinv
        maps.append(m)
    return maps


USE_FUSED = True

def kernel_unfused(I):
    x = np.asarray(I["L0_attn_h_t"], np.float32)
    for layer in range(4):
        nc_ln = _prog("ln", build_ln_phase)
        xn = _unshard_fm(_run(nc_ln, [{"x": _shard_fm(x, c)} for c in range(8)]),
                         "xn")
        if layer == 0:
            y = xn
            for i in range(6):
                nca = _prog("ffn0a_f" if i == 0 else "ffn0a",
                            build_ffn0a_phase, first=(i == 0))
                ymid = _unshard_fm(_run(nca, _ffn0a_maps_k(I, i, y, first=(i == 0))),
                                   "yo")
                ncf = _prog("ffn0f_l" if i == 5 else "ffn0f",
                            build_ffn0f_phase, last=(i == 5))
                y = _unshard_fm(_run(ncf, _ffn0f_maps_k(I, i, ymid, last=(i == 5))),
                                "gg" if i == 5 else "yo")
            gg = y
            nc_at = _prog("attn0", build_attn0_phase)
            x = _unshard_fm(_run(nc_at, _attn0_maps_k(I, x, xn, gg)), "xo")
        elif layer == 1:
            gg = _unshard_fm(_run(_prog("ffn1", build_ffn1_phase),
                                  _ffn1_maps_k(I, xn)), "gg")
            x = _unshard_fm(_run(_prog("attn1", build_attn1_phase),
                                 _attn1_maps_k(I, x, xn, gg)), "xo")
        elif layer == 2:
            gg = _unshard_fm(_run(_prog("ffn2", build_ffn2_phase),
                                  _ffn2_maps_k(I, xn)), "gg")
            x = _unshard_fm(_run(_prog("attn2", build_attn2_phase),
                                 _attn2_maps_k(I, x, xn, gg)), "xo")
        else:
            gg = _unshard_fm(_run(_prog("ffn3", build_ffn3_phase),
                                  _ffn3_maps_k(I, xn)), "gg")
            x = _unshard_fm(_run(_prog("attn3", build_attn3_phase),
                                 _attn3_maps_k(I, x, xn, gg)), "xo")
    return x
